# version 11
# baseline (speedup 1.0000x reference)
import numpy as np
import concourse.bass as bass
import concourse.mybir as mybir
from concourse.bass_utils import run_bass_kernel_spmd
from contextlib import ExitStack

F32 = mybir.dt.float32
I32 = mybir.dt.int32
AF = mybir.ActivationFunctionType
OP = mybir.AluOpType
AX = mybir.AxisListType

NCORES = 8
B_LOC = 2
T = 2048
NT = B_LOC * T
D = 1024
DC = D // 128
N = 16
R = 64
E = 8
FF = 2730
NSLOT = 640
NFC = 22
TC = 256
BIG = 3.0e38

SUBLAYERS = ("mamba", "ffn", "attn", "ffn")


class Prog:
    ENGS = ("tensor", "vector", "scalar", "gpsimd", "sync")

    def __init__(self):
        self.ops = []
        self.last_w = {}
        self.readers = {}
        self.dma_counts = {}
        self.dma_waited = {}
        self.fence_tok = None

    def op(self, eng, fn, reads=(), writes=(), dma=None):
        idx = len(self.ops)
        deps = set()
        dwaits = {}

        def need(t):
            rec = self.ops[t]
            if rec["dma"] is not None:
                s = rec["dma"]
                dwaits[s] = 16 * self.dma_counts[s]
            else:
                deps.add(t)
        if self.fence_tok is not None:
            deps.add(self.fence_tok)
        for k in reads:
            t = self.last_w.get(k)
            if t is not None:
                need(t)
        for k in writes:
            t = self.last_w.get(k)
            if t is not None:
                need(t)
            for t2 in self.readers.get(k, ()):
                need(t2)
        extra_wait = None
        if dma is not None:
            n0 = self.dma_counts.get(dma, 0)
            self.dma_counts[dma] = n0 + 1
            last = self.dma_waited.get(dma, 0)
            if n0 + 1 - last > 8:
                extra_wait = (dma, 16 * n0)
                self.dma_waited[dma] = n0
        if extra_wait is not None:
            dwaits[extra_wait[0]] = max(dwaits.get(extra_wait[0], 0), extra_wait[1])
        self.ops.append(dict(eng=eng, fn=fn, deps=sorted(deps), dma=dma,
                             needed=False, dwaits=dwaits))
        for k in writes:
            self.last_w[k] = idx
            self.readers[k] = []
        for k in reads:
            self.readers.setdefault(k, []).append(idx)
        return idx

    def fence(self):
        keys = list(self.last_w.keys())
        rkeys = [k for k, v in self.readers.items() if v]
        self.fence_tok = self.op("gpsimd", lambda h: h.engine_nop(),
                                 reads=keys, writes=list(set(keys + rkeys)))

    def materialise(self, block, sems, dsems):
        for rec in self.ops:
            for d in rec["deps"]:
                self.ops[d]["needed"] = True
        cnt = {e: 0 for e in self.ENGS}
        for rec in self.ops:
            if rec["dma"] is not None:
                rec["sig"] = ("d", rec["dma"], 16)
            elif rec["needed"]:
                cnt[rec["eng"]] += 1
                rec["sig"] = ("e", rec["eng"], cnt[rec["eng"]])
            else:
                rec["sig"] = None
        marks = {e: {} for e in self.ENGS}
        per_eng = {e: [] for e in self.ENGS}
        for rec in self.ops:
            eng = rec["eng"]
            m = marks[eng]
            waits = []
            for dname, val in sorted(rec["dwaits"].items()):
                key = ("d", dname)
                if m.get(key, 0) >= val:
                    continue
                m[key] = val
                waits.append((dsems[dname], val))
            for d in sorted(rec["deps"]):
                kind, src, val = self.ops[d]["sig"]
                key = (kind, src)
                if m.get(key, 0) >= val:
                    continue
                m[key] = val
                waits.append((sems[src] if kind == "e" else dsems[src], val))
            if rec["sig"] is not None:
                kind, src, val = rec["sig"]
                inc = (sems[src] if kind == "e" else dsems[src],
                       16 if kind == "d" else 1)
            else:
                inc = None
            per_eng[eng].append((waits, rec["fn"], inc))
        for e in self.ENGS:
            def body(h, e=e):
                for waits, fn, inc in per_eng[e]:
                    for sem, val in waits:
                        h.wait_ge(sem, val)
                    inst = fn(h)
                    if inc is not None:
                        inst.then_inc(*inc)
            getattr(block, e)(body)


def build_nc(meta):
    nc = bass.Bass("TRN2", target_bir_lowering=False)
    P = Prog()

    dram = {}
    for name, shape in meta.items():
        dram[name] = nc.dram_tensor(name, list(shape), F32,
                                    kind="ExternalInput").ap()
    y_out = nc.dram_tensor("y", [NT, D], F32, kind="ExternalOutput").ap()
    X = nc.dram_tensor("Xres", [NT + 1, D], F32).ap()
    LNB = nc.dram_tensor("LNbuf", [NT + 1, D], F32).ap()
    CCIN = nc.dram_tensor("ccin", [E, NT], F32).ap()
    CCOUT = nc.dram_tensor("ccout", [NCORES * E, NT], F32).ap()

    ctx = ExitStack()

    def alloc(name, shape, dtype=F32):
        return ctx.enter_context(nc.sbuf_tensor("sb_" + name, list(shape), dtype)).ap()

    def palloc(name, shape):
        return ctx.enter_context(nc.psum_tensor(name, list(shape), F32)).ap()

    with ctx:
        eye = alloc("eye", [128, 128])
        ltri = alloc("ltri", [128, 128])
        ones1 = alloc("ones1", [1, 128])
        maskE = alloc("maskE", [128, 896])
        ids32 = alloc("ids32", [128, 32])
        e_n = alloc("e_n", [16, N * 128])
        esel = alloc("esel", [128, 8])
        eselT = alloc("eselT", [8, 128])
        e8 = alloc("e8", [8, 8 * 128])
        w_xdtT = alloc("w_xdtT", [128, DC, R])
        w_bT = alloc("w_bT", [128, DC, N])
        w_cT = alloc("w_cT", [128, DC, N])
        s_A = alloc("s_A", [128, DC, N])
        b_xdt = alloc("b_xdt", [R, 1])
        b_dt = alloc("b_dt", [128, DC])
        b_B = alloc("b_B", [N, 1])
        b_C = alloc("b_C", [N, 1])
        s_D = alloc("s_D", [128, DC])
        wr = alloc("wr", [128, DC, E])
        br = alloc("br", [1, E])
        bup = alloc("bup", [128, 4, NFC])
        lnm = alloc("lnm", [128, 32])
        lnr = alloc("lnr", [128, 32])
        stat = alloc("stat", [128, 12])
        mv = alloc("mv", [128, 2])
        tiny = alloc("tiny", [128, 16])
        ones32 = alloc("ones32", [128, 32])
        s_sm_st = alloc("s_sm_st", [128, 32, E])
        alloc_gidx = [alloc("gidx_i", [128, 5], I32), alloc("gv_f", [128, 5])]
        xt = [alloc(f"xt{i}", [128, D]) for i in range(2)]
        ut = [alloc(f"ut{i}", [128, D]) for i in range(2)]
        ytm = [alloc(f"ytm{i}", [128, D]) for i in range(2)]
        A = alloc("arenaA", [128, 16384])
        Bb = alloc("arenaB", [128, 16384])
        Cc = alloc("arenaC", [128, 8192])
        w_dtT = Cc[0:R, 6144:7168]

        pA = palloc("pA", [128, 4096])
        p_tr = pA[:, 0:128]
        p_tr2 = pA[:, 512:640]
        p_a = pA[:, 1024:1536]
        p_b = pA[:, 1536:2048]
        p_c = pA[:, 2048:2560]
        p_d = pA[:, 2560:3072]
        p_e = pA[:, 3072:3584]
        p_f = pA[:, 3584:4096]
        p_mid = pA[:, 3072:3712]
        p_dn = pA[:, 1024:3584].rearrange("p (s h) -> p s h", s=5)

        sems = {e: ctx.enter_context(nc.semaphore(f"ps_{e}")) for e in Prog.ENGS}
        dnames = ["wload", "xin", "xrd", "xrd2", "xout", "wu0", "wu1",
                  "wd0", "wd1", "wbd", "lnw", "gat", "scat", "ccio", "yout",
                  "xs0", "xs1", "misc"]
        dsems = {n: ctx.enter_context(nc.semaphore(f"ds_{n}")) for n in dnames}
        ccsem = ctx.enter_context(nc.semaphore("ccsem"))

        block = ctx.enter_context(nc.Block())

        def dma(qeng, sem, out, in_, reads, writes, accum=False):
            def fn(h, out=out, in_=in_, accum=accum):
                if accum:
                    return h.dma_start(out=out, in_=in_, accum_op=OP.add)
                return h.dma_start(out=out, in_=in_)
            return P.op(qeng, fn, reads=reads, writes=writes, dma=sem)

        for dst, srcname, key in [(eye, "c_eye", "eye"), (ltri, "c_ltri", "ltri"),
                             (maskE, "c_maskE", "maskE"),
                             (ids32, "c_ids", "ids32"), (e_n, "c_en", "e_n"),
                             (esel, "c_esel", "esel"), (eselT, "c_eselT", "eselT"),
                             (e8, "c_e8", "e8")]:
            dma("sync", "wload", dst, dram[srcname], [], [key])
        P.op("vector", lambda h: h.memset(ones1, 1.0), [], ["ones1"])
        P.op("vector", lambda h: h.memset(ones32, 1.0), [], ["ones32"])
        dma("sync", "wload", w_xdtT,
            dram["w_xdtT"].rearrange("(c p) r -> p c r", p=128), [], ["w_xdtT"])
        dma("sync", "wload", w_dtT, dram["w_dtT"], [], ["m_wdtT"])
        dma("sync", "wload", w_bT,
            dram["w_bT"].rearrange("(c p) n -> p c n", p=128), [], ["w_bT"])
        dma("sync", "wload", w_cT,
            dram["w_cT"].rearrange("(c p) n -> p c n", p=128), [], ["w_cT"])
        dma("sync", "wload", s_A,
            dram["L0_attn_a_A_log"].rearrange("(c p) n -> p c n", p=128),
            [], ["s_A"])
        dma("sync", "wload", b_xdt, dram["L0_attn_a_b_xdt"].unsqueeze(1),
            [], ["b_xdt"])
        dma("sync", "wload", b_dt, dram["c_bdt_fm"], [], ["b_dt"])
        dma("sync", "wload", b_B, dram["L0_attn_a_b_B"].unsqueeze(1), [], ["b_B"])
        dma("sync", "wload", b_C, dram["L0_attn_a_b_C"].unsqueeze(1), [], ["b_C"])
        dma("sync", "wload", s_D, dram["c_d_fm"], [], ["s_D"])
        dma("sync", "wload", wr,
            dram["L0_ffn_Wr"].rearrange("(c p) e -> p c e", p=128), [], ["wr"])
        dma("sync", "wload", br, dram["L0_ffn_br"].unsqueeze(0), [], ["br"])
        dma("sync", "wload", bup, dram["c_bup_fm"], [], ["bup"])
        P.op("scalar", lambda h: h.activation(s_A, s_A, AF.Exp), ["s_A"], ["s_A"])
        P.op("vector", lambda h: h.tensor_scalar(s_A, s_A, -1.0, None,
                                                 op0=OP.mult), ["s_A"], ["s_A"])
        dma("sync", "xin", X[0:1, :], dram["c_zrow"], [], ["X"])
        dma("sync", "xin", X[1:NT + 1, :], dram["x0"], [], ["X"])
        dma("sync", "xin", LNB[0:1, :], dram["c_zrow"], [], ["LNB"])

        def ln_stats_all():
            for c in range(32):
                slot = c % 2
                xk = f"xt{slot}"
                dma("sync", f"xs{slot}", xt[slot],
                    X[1 + 128 * c:1 + 128 * (c + 1), :], ["X"], [xk])
                P.op("vector", lambda h, s=slot: h.bn_stats(stat[:, 0:6],
                     xt[s][:, 0:512]), [xk], ["stat"])
                P.op("vector", lambda h, s=slot: h.bn_stats(stat[:, 6:12],
                     xt[s][:, 512:1024]), [xk], ["stat"])
                P.op("vector", lambda h: h.bn_aggr(mv, stat), ["stat"], ["mv"])
                P.op("vector", lambda h, c=c: h.tensor_scalar(
                    lnr[:, c:c + 1], mv[:, 1:2], 1e-5, None, op0=OP.add),
                    ["mv"], ["lnr"])
                P.op("vector", lambda h, c=c: h.reciprocal(
                    lnr[:, c:c + 1], lnr[:, c:c + 1]), ["lnr"], ["lnr"])
                P.op("scalar", lambda h, c=c: h.activation(
                    lnr[:, c:c + 1], lnr[:, c:c + 1], AF.Sqrt), ["lnr"], ["lnr"])
                P.op("vector", lambda h, c=c: h.tensor_tensor(
                    lnm[:, c:c + 1], mv[:, 0:1], lnr[:, c:c + 1], op=OP.mult),
                    ["mv", "lnr"], ["lnm"])
                P.op("vector", lambda h, c=c: h.tensor_scalar(
                    lnm[:, c:c + 1], lnm[:, c:c + 1], -1.0, None, op0=OP.mult),
                    ["lnm"], ["lnm"])

        def u_token_tile(c, slot):
            xk = f"xt{slot}"
            uk = f"ut{slot}"
            dma("sync", f"xs{slot}", xt[slot],
                X[1 + 128 * c:1 + 128 * (c + 1), :], ["X"], [xk])
            P.op("scalar", lambda h, c=c, s=slot: h.activation(
                ut[s], xt[s], AF.Identity, bias=lnm[:, c:c + 1],
                scale=lnr[:, c:c + 1]), [xk, "lnm", "lnr"], [uk])
            return uk

        def transpose_to(src_ap, src_key, dst_ap, dst_key, pi):
            pb, pk = (p_tr, "pb0") if pi % 2 == 0 else (p_tr2, "pb1")
            P.op("tensor", lambda h: h.transpose(pb, src_ap, eye),
                 [src_key, "eye"], [pk])
            P.op("scalar", lambda h: h.copy(dst_ap, pb), [pk], [dst_key])

        def emit_mamba():
            ln_stats_all()
            for s in range(B_LOC):
                base = s * T
                ufm = A[:, 0:DC * 512].rearrange("p (c t) -> p c t", c=DC)
                xdt = Cc[:, 0:2048]
                bt = Cc[:, 2048:4096]
                ct = Cc[:, 4096:6144]
                for blk in range(T // 512):
                    t0 = 512 * blk
                    for tb in range(4):
                        c = (base + t0) // 128 + tb
                        slot = tb % 2
                        uk = u_token_tile(c, slot)
                        for d in range(DC):
                            transpose_to(ut[slot][:, 128 * d:128 * (d + 1)], uk,
                                         ufm[:, d, 128 * tb:128 * (tb + 1)],
                                         "mA_ufm", tb * DC + d)
                    def mk_mm(psum, wap, wkey, nrows):
                        def f(h):
                            for d in range(DC):
                                r = h.matmul(psum, wap[:, d, :],
                                             ufm[:, d, 0:512],
                                             start=(d == 0), stop=(d == DC - 1))
                            return r
                        return f
                    P.op("tensor", mk_mm(p_a[0:R, :], w_xdtT, "w_xdtT", R),
                         ["mA_ufm", "w_xdtT"], ["p_a"])
                    P.op("scalar", lambda h, t0=t0: h.activation(
                        xdt[0:R, t0:t0 + 512], p_a[0:R, :], AF.Identity,
                        bias=b_xdt), ["p_a", "b_xdt"], ["m_xdt"])
                    P.op("tensor", mk_mm(p_b[0:N, :], w_bT, "w_bT", N),
                         ["mA_ufm", "w_bT"], ["p_b"])
                    P.op("scalar", lambda h, t0=t0: h.activation(
                        bt[0:N, t0:t0 + 512], p_b[0:N, :], AF.Identity,
                        bias=b_B), ["p_b", "b_B"], ["m_bt"])
                    P.op("tensor", mk_mm(p_c[0:N, :], w_cT, "w_cT", N),
                         ["mA_ufm", "w_cT"], ["p_c"])
                    P.op("scalar", lambda h, t0=t0: h.activation(
                        ct[0:N, t0:t0 + 512], p_c[0:N, :], AF.Identity,
                        bias=b_C), ["p_c", "b_C"], ["m_ct"])
                P.fence()
                ab = [A[:, 0:4096].rearrange("p (n t) -> p n t", n=N),
                      A[:, 4096:8192].rearrange("p (n t) -> p n t", n=N)]
                xs3 = A[:, 8192:12288].rearrange("p (n t) -> p n t", n=N)
                h3 = A[:, 12288:16384].rearrange("p (n t) -> p n t", n=N)
                wv_o = [A[:, 0:4096].rearrange("p (t n) -> p n t", n=N),
                        A[:, 4096:8192].rearrange("p (t n) -> p n t", n=N)]
                wv_r = [A[:, 0:4096].rearrange("p (t n) -> p t n", n=N),
                        A[:, 4096:8192].rearrange("p (t n) -> p t n", n=N)]
                bbcs = [Bb[:, 0:4096].rearrange("p (n t) -> p n t", n=N),
                        Bb[:, 8192:12288].rearrange("p (n t) -> p n t", n=N)]
                cbcs = [Bb[:, 4096:8192].rearrange("p (n t) -> p n t", n=N),
                        Bb[:, 12288:16384].rearrange("p (n t) -> p n t", n=N)]
                udcs = [Cc[:, 7168:7424], ut[1][:, 256:512]]
                dtu = Cc[:, 7424:7680]
                dts = [Cc[:, 7680:7936], Cc[:, 7936:8192]]
                red = ut[0][:, 256:512]
                yfm = ut[0][:, 512:768]
                hc = ut[0][:, 768:768 + DC * N].rearrange("p (c n) -> p c n", c=DC)
                P.op("vector", lambda h: h.memset(
                    ut[0][:, 768:768 + DC * N], 0.0), [], ["m_hc"])
                NG = (T // TC) * DC

                def prep(gi):
                    tci, d = divmod(gi, DC)
                    t0 = tci * TC
                    par = tci % 2
                    g = gi % 2
                    dt_, dtk = dts[g], f"m_dt{g}"
                    abg, abk = ab[g], f"m_ab{g}"
                    udc, udck = udcs[g], f"m_udc{g}"
                    if d == 0:
                        bbc, cbc = bbcs[par], cbcs[par]
                        bbck, cbck = f"m_bbc{par}", f"m_cbc{par}"
                        for srck, src_, dst, dstk in (("m_bt", bt, bbc, bbck),
                                                      ("m_ct", ct, cbc, cbck)):
                            for n in range(N):
                                pb, pk = (p_c, "pb4") if n % 2 == 0 else (p_d, "pb5")
                                P.op("tensor", lambda h, n=n, src_=src_, t0=t0, pb=pb:
                                     h.matmul(pb[:, 0:TC],
                                              e_n[:, 128 * n:128 * (n + 1)],
                                              src_[0:N, t0:t0 + TC],
                                              start=True, stop=True),
                                     [srck, "e_n"], [pk])
                                P.op("scalar", lambda h, n=n, dst=dst, pb=pb:
                                     h.copy(dst[:, n, :], pb[:, 0:TC]),
                                     [pk], [dstk])
                    for j in range(TC // 128):
                        cglob = (base + t0) // 128 + j
                        dma("sync", "xrd" if j % 2 == 0 else "xrd2",
                            xt[j % 2][:, 0:128],
                            X[1 + 128 * cglob:1 + 128 * (cglob + 1),
                              128 * d:128 * (d + 1)],
                            ["X"], [f"xt{j % 2}"])
                        P.op("scalar", lambda h, c=cglob, j=j: h.activation(
                            ut[j % 2][:, 0:128], xt[j % 2][:, 0:128],
                            AF.Identity, bias=lnm[:, c:c + 1],
                            scale=lnr[:, c:c + 1]),
                            [f"xt{j % 2}", "lnm", "lnr"], [f"ut{j % 2}"])
                        transpose_to(ut[j % 2][:, 0:128], f"ut{j % 2}",
                                     udc[:, 128 * j:128 * (j + 1)], udck, j)
                    pdt, pdtk = (p_a, "pb2") if g == 0 else (p_b, "pb3")
                    P.op("tensor", lambda h, d=d, t0=t0, pdt=pdt: h.matmul(
                        pdt[:, 0:TC], w_dtT[:, 128 * d:128 * (d + 1)],
                        xdt[0:R, t0:t0 + TC], start=True, stop=True),
                        ["m_xdt", "m_wdtT"], [pdtk])
                    P.op("scalar", lambda h, d=d, dt_=dt_, pdt=pdt: h.activation(
                        dt_, pdt[:, 0:TC], AF.Exp, bias=b_dt[:, d:d + 1]),
                        [pdtk, "b_dt"], [dtk])
                    P.op("scalar", lambda h, dt_=dt_: h.activation(
                        dt_, dt_, AF.Ln, bias=1.0), [dtk], [dtk])
                    abf = abg.rearrange("p n t -> p (n t)")
                    P.op("vector", lambda h, d=d, dt_=dt_, abg=abg:
                         h.tensor_tensor(
                             abg,
                             s_A[:, d, :].unsqueeze(2).to_broadcast([128, N, TC]),
                             dt_.unsqueeze(1).to_broadcast([128, N, TC]),
                             op=OP.mult), [dtk, "s_A"], [abk])
                    P.op("scalar", lambda h, abf=abf: h.activation(
                         abf, abf, AF.Exp), [abk], [abk])

                def compute(gi):
                    tci, d = divmod(gi, DC)
                    t0 = tci * TC
                    par = tci % 2
                    g = gi % 2
                    dt_, dtk = dts[g], f"m_dt{g}"
                    abg, abk = ab[g], f"m_ab{g}"
                    udc, udck = udcs[g], f"m_udc{g}"
                    bbc, cbc = bbcs[par], cbcs[par]
                    bbck, cbck = f"m_bbc{par}", f"m_cbc{par}"
                    P.op("gpsimd", lambda h, dt_=dt_, udc=udc: h.tensor_tensor(
                        dtu, dt_, udc, op=OP.mult), [dtk, udck], ["m_dtu"])
                    P.op("gpsimd", lambda h, bbc=bbc: h.tensor_tensor(
                        xs3, bbc, dtu.unsqueeze(1).to_broadcast([128, N, TC]),
                        op=OP.mult), [bbck, "m_dtu"], ["m_xs"])
                    P.op("vector", lambda h, d=d, abg=abg: h.tensor_tensor(
                        tiny, abg[:, :, 0], hc[:, d, :], op=OP.mult),
                        [abk, "m_hc"], ["tiny"])
                    P.op("vector", lambda h: h.tensor_tensor(
                        xs3[:, :, 0], xs3[:, :, 0], tiny, op=OP.add),
                        ["tiny", "m_xs"], ["m_xs"])
                    P.op("vector", lambda h, abg=abg: h.memset(
                        abg[:, :, 0], 0.0), [abk], [abk])
                    P.op("vector", lambda h, abg=abg: h.tensor_tensor_scan(
                        h3.rearrange("p n t -> p (n t)"),
                        abg.rearrange("p n t -> p (n t)"),
                        xs3.rearrange("p n t -> p (n t)"),
                        0.0, op0=OP.mult, op1=OP.add),
                        [abk, "m_xs"], ["m_h"])
                    P.op("vector", lambda h, d=d: h.tensor_copy(
                        hc[:, d, :], h3[:, :, TC - 1]), ["m_h"], ["m_hc"])
                    P.op("gpsimd", lambda h, g=g, cbc=cbc: h.tensor_tensor(
                        wv_o[g], h3, cbc, op=OP.mult), ["m_h", cbck], [abk])
                    P.op("vector", lambda h, g=g: h.reduce_sum(
                        red, wv_r[g], axis=AX.X), [abk], ["m_red"])
                    P.op("vector", lambda h, d=d, udc=udc: h.scalar_tensor_tensor(
                        yfm, udc, s_D[:, d:d + 1], red, op0=OP.mult,
                        op1=OP.add), [udck, "s_D", "m_red"], ["m_yfm"])
                    for j in range(TC // 128):
                        transpose_to(yfm[:, 128 * j:128 * (j + 1)], "m_yfm",
                                     ytm[j][:, 128 * d:128 * (d + 1)],
                                     f"ytm{j}", d * 2 + j)
                    if d == DC - 1:
                        for j in range(TC // 128):
                            r0 = 1 + base + t0 + 128 * j
                            dma("gpsimd", "xout", X[r0:r0 + 128, :], ytm[j],
                                [f"ytm{j}"], ["X"], accum=True)

                prep(0)
                for gi in range(NG):
                    if gi + 1 < NG:
                        prep(gi + 1)
                    compute(gi)
                P.fence()

            P.fence()

        FCS = [128] * 21 + [42]

        def emit_ffn(call):
            ln_stats_all()
            ufm = A[:, 0:1024].rearrange("p (d t) -> p d t", d=DC)
            smT = Cc[0:8, 0:4096]
            s_sm = s_sm_st
            for c in range(32):
                slot = c % 2
                uk = u_token_tile(c, slot)
                dma("gpsimd", "lnw", LNB[1 + 128 * c:1 + 128 * (c + 1), :],
                    ut[slot], [uk], ["LNB"])
                for d in range(DC):
                    transpose_to(ut[slot][:, 128 * d:128 * (d + 1)], uk,
                                 ufm[:, d, :], "f_ufm", d)
                def mm_rt(h):
                    for d in range(DC):
                        h.matmul(p_a[:, 0:E], ufm[:, d, :], wr[:, d, :],
                                 start=(d == 0), stop=False)
                    return h.matmul(p_a[:, 0:E], ones1, br, start=False,
                                    stop=True)
                P.op("tensor", mm_rt, ["f_ufm", "wr", "br", "ones1"], ["pb2"])
                P.op("vector", lambda h: h.reduce_max(mv[:, 0:1], p_a[:, 0:E],
                     axis=AX.X), ["pb2"], ["mv"])
                P.op("vector", lambda h: h.tensor_scalar(mv[:, 1:2], mv[:, 0:1],
                     -1.0, None, op0=OP.mult), ["mv"], ["mv"])
                P.op("scalar", lambda h: h.activation(stat[:, 0:E], p_a[:, 0:E],
                     AF.Exp, bias=mv[:, 1:2], accum_out=mv[:, 0:1]),
                     ["pb2", "mv"], ["stat", "mv"])
                P.op("vector", lambda h: h.reciprocal(mv[:, 1:2], mv[:, 0:1]),
                     ["mv"], ["mv"])
                P.op("vector", lambda h, c=c: h.tensor_scalar(s_sm[:, c, :],
                     stat[:, 0:E], mv[:, 1:2], None, op0=OP.mult),
                     ["stat", "mv"], ["f_sm"])
                P.op("tensor", lambda h, c=c: h.transpose(p_tr2[0:E, :],
                     s_sm[:, c, :], eye), ["f_sm", "eye"], ["pb1"])
                P.op("scalar", lambda h, c=c: h.copy(smT[:, 128 * c:128 * (c + 1)],
                     p_tr2[0:E, :]), ["pb1"], ["f_smT"])
            dma("gpsimd", "ccio", CCIN, smT, ["f_smT"], ["CCIN"])
            def cc(h):
                return h.collective_compute(
                    "AllGather", OP.bypass,
                    replica_groups=[list(range(NCORES))],
                    ins=[CCIN], outs=[CCOUT])
            P.op("gpsimd", cc, ["CCIN"], ["CCOUT"])
            smA = A[:, 4096:6144]
            dma("gpsimd", "ccio", smA,
                CCOUT.rearrange("(r e) (hh t) -> (r e hh) t", r=NCORES, hh=2),
                ["CCOUT"], ["f_smA"])
            bis = Cc[0:8, 4096:4104]
            lo8, hi8, tau8 = bis[:, 0:1], bis[:, 1:2], bis[:, 2:3]
            cnt8, m8, d18 = bis[:, 3:4], bis[:, 4:5], bis[:, 5:6]
            c128 = mv[:, 0:1]
            P.op("vector", lambda h: h.memset(lo8, 0.0), [], ["f_bis"])
            P.op("vector", lambda h: h.memset(hi8, 1.0), [], ["f_bis"])
            P.op("vector", lambda h: h.memset(tau8, 0.5), [], ["f_bis"])
            fcmp = Bb[:, 0:2048]
            for it in range(30):
                P.op("tensor", lambda h: h.matmul(p_e[:, 0:1], eselT, tau8,
                     start=True, stop=True), ["f_bis", "eselT"], ["pb6"])
                P.op("vector", lambda h: h.tensor_scalar(fcmp, smA, p_e[:, 0:1],
                     0.0, op0=OP.is_gt, op1=OP.add, accum_out=c128),
                     ["f_smA", "pb6"], ["f_cmp", "mv"])
                P.op("tensor", lambda h: h.matmul(p_f[0:8, 0:1], esel, c128,
                     start=True, stop=True), ["mv", "esel"], ["pb7"])
                P.op("vector", lambda h: h.tensor_copy(cnt8, p_f[0:8, 0:1]),
                     ["pb7"], ["f_bis"])
                P.op("vector", lambda h: h.tensor_scalar(m8, cnt8,
                     float(NT), None, op0=OP.is_ge), ["f_bis"], ["f_bis"])
                P.op("vector", lambda h: h.tensor_tensor(d18, tau8, lo8,
                     op=OP.subtract), ["f_bis"], ["f_bis"])
                P.op("vector", lambda h: h.tensor_tensor(d18, d18, m8,
                     op=OP.mult), ["f_bis"], ["f_bis"])
                P.op("vector", lambda h: h.tensor_tensor(lo8, lo8, d18,
                     op=OP.add), ["f_bis"], ["f_bis"])
                P.op("vector", lambda h: h.tensor_tensor(d18, tau8, hi8,
                     op=OP.subtract), ["f_bis"], ["f_bis"])
                P.op("vector", lambda h: h.tensor_tensor(d18, d18, m8,
                     op=OP.mult), ["f_bis"], ["f_bis"])
                P.op("vector", lambda h: h.tensor_tensor(hi8, tau8, d18,
                     op=OP.subtract), ["f_bis"], ["f_bis"])
                P.op("vector", lambda h: h.tensor_tensor(tau8, lo8, hi8,
                     op=OP.add), ["f_bis"], ["f_bis"])
                P.op("vector", lambda h: h.tensor_scalar(tau8, tau8, 0.5,
                     None, op0=OP.mult), ["f_bis"], ["f_bis"])
            P.fence()
            iota = Bb[:, 15360:16000]
            dma("sync", "wload", iota, dram["c_iota640"], [], ["f_iota"])
            ug = Bb[:, 0:5120].rearrange("p (j d) -> p j d", j=5)
            ugfm = Bb[:, 5120:10240].rearrange("p (d t) -> p d t", d=DC)
            ys = Bb[:, 10240:15360].rearrange("p (j d) -> p j d", j=5)
            rhs2 = Bb[:, 16064:16128].rearrange("p (c k) -> p c k", c=32)
            mid = A[:, 0:NFC * 640].rearrange("p (f t) -> p f t", f=NFC)
            ohb = [A[:, 14080:14720], A[:, 14720:15360]]
            wub = [Cc[:, 0:1024].rearrange("p (d f) -> p d f", d=DC),
                   Cc[:, 1024:2048].rearrange("p (d f) -> p d f", d=DC)]
            wdb = [Cc[:, 2048:2560], Cc[:, 2560:3072]]
            bdn1 = Cc[0:1, 3072:4096]
            s_msk = Bb[:, 16000:16032]
            s_pfx = Bb[:, 16032:16064]
            s_tot = tiny[:, 0:1]
            s_gidx = alloc_gidx[0]
            s_gv = alloc_gidx[1]
            for e in range(E):
                g = e // 2
                P.op("tensor", lambda h, e=e: h.matmul(p_e[:, 0:1],
                     e8[:, 128 * e:128 * (e + 1)], tau8, start=True, stop=True),
                     ["f_bis", "e8"], ["pb6"])
                P.op("vector", lambda h, e=e: h.tensor_scalar(s_msk,
                     s_sm[:, :, e], p_e[:, 0:1], None, op0=OP.is_gt),
                     ["f_sm", "pb6"], ["f_msk"])
                P.op("vector", lambda h: h.tensor_tensor_scan(s_pfx, ones32,
                     s_msk, 0.0, op0=OP.mult, op1=OP.add),
                     ["f_msk", "ones32"], ["f_pfx"])
                P.op("vector", lambda h: h.reduce_sum(s_tot, s_msk, axis=AX.X),
                     ["f_msk"], ["tiny"])
                P.op("tensor", lambda h: h.matmul(p_f[:, 0:1], ltri, s_tot,
                     start=True, stop=True), ["tiny", "ltri"], ["pb7"])
                P.op("vector", lambda h: h.tensor_scalar(s_pfx, s_pfx,
                     p_f[:, 0:1], -1.0, op0=OP.add, op1=OP.add),
                     ["f_pfx", "pb7"], ["f_pfx"])
                P.op("vector", lambda h: h.scalar_tensor_tensor(s_pfx, s_pfx,
                     1.0, s_msk, op0=OP.add, op1=OP.mult),
                     ["f_pfx", "f_msk"], ["f_pfx"])
                P.op("vector", lambda h: h.tensor_scalar(s_pfx, s_pfx, -1.0,
                     None, op0=OP.add), ["f_pfx"], ["f_pfx"])
                P.op("vector", lambda h: h.tensor_copy(rhs2[:, :, 0:1],
                     ids32.unsqueeze(2)), ["ids32"], ["f_rhs2"])
                P.op("vector", lambda h, e=e: h.tensor_copy(rhs2[:, :, 1:2],
                     s_sm[:, :, e].unsqueeze(2)), ["f_sm"], ["f_rhs2"])
                for c in range(32):
                    ob = ohb[c % 2]
                    obk = f"f_oh{c % 2}"
                    P.op("vector", lambda h, c=c, ob=ob: h.tensor_scalar(
                         ob, iota, s_pfx[:, c:c + 1], None, op0=OP.is_equal),
                         ["f_iota", "f_pfx"], [obk])
                    def mm_g(h, c=c, ob=ob):
                        for sbk in range(5):
                            r = h.matmul(p_a[:, 2 * sbk:2 * sbk + 2],
                                         ob[:, 128 * sbk:128 * (sbk + 1)],
                                         rhs2[:, c, :],
                                         start=(c == 0 and sbk == 0),
                                         stop=(c == 31 and sbk == 4))
                        return r
                    P.op("tensor", mm_g, [obk, "f_rhs2"], ["pb2"])
                P.op("vector", lambda h: h.tensor_copy(
                     s_gidx, p_a[:, 0:10].rearrange("p (s k) -> p s k", k=2)[:, :, 0]),
                     ["pb2"], ["f_gidx"])
                P.op("vector", lambda h: h.tensor_copy(
                     s_gv, p_a[:, 0:10].rearrange("p (s k) -> p s k", k=2)[:, :, 1]),
                     ["pb2"], ["f_gv"])
                for j in range(5):
                    def gath(h, j=j):
                        return h.indirect_dma_start(
                            out=ug[:, j, :], out_offset=None, in_=LNB,
                            in_offset=bass.IndirectOffsetOnAxis(
                                ap=s_gidx[:, j:j + 1], axis=0))
                    P.op("gpsimd", gath, ["LNB", "f_gidx"], ["f_ug"], dma="gat")
                for j in range(5):
                    for d in range(DC):
                        transpose_to(ug[:, j, 128 * d:128 * (d + 1)], "f_ug",
                                     ugfm[:, d, 128 * j:128 * (j + 1)],
                                     "f_ugfm", j * DC + d)
                p_mid2 = pA[:, 2048:2688]
                for fc in range(NFC):
                    wb = wub[fc % 2]
                    wbk = f"f_wu{fc % 2}"
                    fcs = FCS[fc]
                    pm = p_mid if fc % 2 == 0 else p_mid2
                    pmk = ["pb6", "pb7"] if fc % 2 == 0 else ["pb4", "pb5"]
                    dma("sync", f"wu{fc % 2}", wb[:, :, 0:fcs],
                        dram["L0_ffn_W_up"][g, :, 128 * fc:128 * fc + fcs]
                        .rearrange("(d p) f -> p d f", p=128), [], [wbk])
                    def mm_up(h, wb=wb, fcs=fcs, pm=pm, m0=0, m1=512):
                        for d in range(DC):
                            r = h.matmul(pm[0:fcs, m0:m1],
                                         wb[:, d, 0:fcs], ugfm[:, d, m0:m1],
                                         start=(d == 0), stop=(d == DC - 1))
                        return r
                    P.op("tensor", lambda h, f=mm_up: f(h),
                         [wbk, "f_ugfm"], [pmk[0]])
                    P.op("tensor", lambda h, wb=wb, fcs=fcs, pm=pm: mm_up(
                         h, wb=wb, fcs=fcs, pm=pm, m0=512, m1=640),
                         [wbk, "f_ugfm"], [pmk[1]])
                    P.op("scalar", lambda h, fc=fc, fcs=fcs, g=g, pm=pm: h.activation(
                         mid[0:fcs, fc, :], pm[0:fcs, :], AF.Silu,
                         bias=bup[0:fcs, g, fc:fc + 1]),
                         pmk + ["bup"], ["f_mid"])
                dma("sync", "wbd", bdn1,
                    dram["L0_ffn_b_down"][e:e + 1, :], [], ["f_bdn"])
                for hh in range(2):
                    for fc in range(NFC):
                        wb = wdb[fc % 2]
                        wbk = f"f_wd{fc % 2}"
                        fcs = FCS[fc]
                        dma("sync", f"wd{fc % 2}", wb[0:fcs, :],
                            dram["L0_ffn_W_down"][e, 128 * fc:128 * fc + fcs,
                                                  512 * hh:512 * (hh + 1)],
                            [], [wbk])
                        def mm_dn(h, wb=wb, fcs=fcs, fc=fc):
                            for ts in range(5):
                                r = h.matmul(p_dn[:, ts, :],
                                             mid[0:fcs, fc, 128 * ts:128 * (ts + 1)],
                                             wb[0:fcs, :],
                                             start=(fc == 0), stop=False)
                            return r
                        P.op("tensor", mm_dn, [wbk, "f_mid"],
                             [f"pb{2 + ts2}" for ts2 in range(5)])
                    def mm_bias(h, hh=hh):
                        for ts in range(5):
                            r = h.matmul(p_dn[:, ts, :], ones1,
                                         bdn1[:, 512 * hh:512 * (hh + 1)],
                                         start=False, stop=True)
                        return r
                    P.op("tensor", mm_bias, ["f_bdn", "ones1"],
                         [f"pb{2 + ts2}" for ts2 in range(5)])
                    for ts in range(5):
                        P.op("scalar", lambda h, ts=ts, hh=hh: h.activation(
                             ys[:, ts, 512 * hh:512 * (hh + 1)], p_dn[:, ts, :],
                             AF.Identity, scale=s_gv[:, ts:ts + 1]),
                             [f"pb{2 + ts}", "f_gv"], ["f_ys"])
                for j in range(5):
                    def scat(h, j=j):
                        return h.indirect_dma_start(
                            out=X, out_offset=bass.IndirectOffsetOnAxis(
                                ap=s_gidx[:, j:j + 1], axis=0),
                            in_=ys[:, j, :], in_offset=None,
                            compute_op=OP.add)
                    P.op("gpsimd", scat, ["f_ys", "f_gidx"], ["X"], dma="scat")
            P.fence()

        def emit_attn():
            ln_stats_all()
            K_fm = A.rearrange("p (hc t) -> p hc t", hc=DC)
            V_tm = Bb.rearrange("p (tt h) -> p tt h", tt=16)
            scr = Cc[:, 0:2048]
            wtb = [Cc[:, 2048:2176], Cc[:, 2176:2304]]
            q_fm = Cc[:, 3072:4096].rearrange("p (d t) -> p d t", d=DC)
            ufm_q = Cc[:, 4096:5120].rearrange("p (d t) -> p d t", d=DC)
            ufm_b = Cc[:, 4096:8192].rearrange("p (d t) -> p d t", d=DC)
            wbuf1 = [Cc[:, 2048:3072].rearrange("p (d h) -> p d h", d=DC),
                     Cc[:, 3072:4096].rearrange("p (d h) -> p d h", d=DC)]
            wbuf2 = [Cc[:, 5120:6144].rearrange("p (d h) -> p d h", d=DC),
                     Cc[:, 6144:7168].rearrange("p (d h) -> p d h", d=DC)]
            p_av = pA[:, 2048:3072]
            CSC = 1.0 / 32.0
            for s in range(B_LOC):
                base = s * T
                for kb in range(4):
                    t0 = 512 * kb
                    for tb in range(4):
                        c = (base + t0) // 128 + tb
                        slot = tb % 2
                        uk = u_token_tile(c, slot)
                        for d in range(DC):
                            transpose_to(ut[slot][:, 128 * d:128 * (d + 1)], uk,
                                         ufm_b[:, d, 128 * tb:128 * (tb + 1)],
                                         "a_ufb", tb * DC + d)
                    for hc in range(DC):
                        wb, wbk = wbuf1[hc % 2], f"a_w{hc % 2}"
                        dma("sync", f"wu{hc % 2}", wb,
                            dram["L0_attn_b_Wk"][:, 128 * hc:128 * (hc + 1)]
                            .rearrange("(d p) h -> p d h", p=128), [], [wbk])
                        pb, pk = (p_a, "pb2") if hc % 2 == 0 else (p_b, "pb3")
                        def mm_k(h, wb=wb, pb=pb):
                            for d in range(DC):
                                r = h.matmul(pb, wb[:, d, :], ufm_b[:, d, :],
                                             start=(d == 0), stop=(d == DC - 1))
                            return r
                        P.op("tensor", mm_k, [wbk, "a_ufb"], [pk])
                        P.op("scalar", lambda h, hc=hc, t0=t0, pb=pb: h.copy(
                             K_fm[:, hc, t0:t0 + 512], pb), [pk], ["a_K"])
                    for hh in range(2):
                        for d in range(DC):
                            wb, wbk = wbuf1[d % 2], f"a_w{d % 2}"
                            dma("sync", f"wu{d % 2}",
                                wb.rearrange("p d h -> p (d h)")[:, 0:512],
                                dram["L0_attn_b_Wv"][128 * d:128 * (d + 1),
                                                     512 * hh:512 * (hh + 1)],
                                [], [wbk])
                            def mm_v(h, wb=wb, d=d):
                                for tt in range(4):
                                    r = h.matmul(
                                        p_dn[:, tt, :],
                                        ufm_b[:, d, 128 * tt:128 * (tt + 1)],
                                        wb.rearrange("p d h -> p (d h)")[:, 0:512],
                                        start=(d == 0), stop=(d == DC - 1))
                                return r
                            P.op("tensor", mm_v, [wbk, "a_ufb"],
                                 [f"pb{2 + tt2}" for tt2 in range(4)])
                        for tt in range(4):
                            P.op("scalar", lambda h, kb=kb, tt=tt, hh=hh: h.copy(
                                 V_tm[:, 4 * kb + tt, 512 * hh:512 * (hh + 1)],
                                 p_dn[:, tt, :]),
                                 [f"pb{2 + tt}"], ["a_V"])
                P.fence()
                def qprep(qi):
                    cq = (base + 128 * qi) // 128
                    slot = qi % 2
                    uk = u_token_tile(cq, slot)
                    for d in range(DC):
                        transpose_to(ut[slot][:, 128 * d:128 * (d + 1)], uk,
                                     ufm_q[:, d, :], "a_uq", d)
                    for hc in range(DC):
                        wb, wbk = wbuf2[hc % 2], f"a_w{hc % 2}"
                        dma("sync", f"wu{hc % 2}", wb,
                            dram["L0_attn_b_Wq"][:, 128 * hc:128 * (hc + 1)]
                            .rearrange("(d p) h -> p d h", p=128), [], [wbk])
                        pb, pk = (p_a, "pb2") if hc % 2 == 0 else (p_b, "pb3")
                        def mm_q(h, wb=wb, pb=pb):
                            for d in range(DC):
                                r = h.matmul(pb[:, 0:128], wb[:, d, :],
                                             ufm_q[:, d, :],
                                             start=(d == 0), stop=(d == DC - 1))
                            return r
                        P.op("tensor", mm_q, [wbk, "a_uq"], [pk])
                        P.op("scalar", lambda h, hc=hc, pb=pb: h.copy(
                             q_fm[:, hc, :], pb[:, 0:128]), [pk], ["a_Q"])

                def qscores(qi):
                    nk = qi // 4 + 1
                    pbs = [p_a, p_b, p_c, p_d]
                    def mm_s(h, nk=nk):
                        for hc in range(DC):
                            for kb in range(nk):
                                r = h.matmul(pbs[kb], q_fm[:, hc, :],
                                             K_fm[:, hc, 512 * kb:512 * (kb + 1)],
                                             start=(hc == 0), stop=(hc == DC - 1))
                        return r
                    P.op("tensor", mm_s, ["a_Q", "a_K"],
                         [f"pb{2 + kb2}" for kb2 in range(nk)])
                    for kb in range(nk):
                        pb, pk = pbs[kb], f"pb{2 + kb}"
                        if kb == nk - 1:
                            moff = 384 - 128 * (qi % 4)
                            P.op("vector", lambda h, kb=kb, pb=pb, moff=moff:
                                 h.tensor_tensor(scr[:, 512 * kb:512 * (kb + 1)],
                                                 pb, maskE[:, moff:moff + 512],
                                                 op=OP.add),
                                 [pk, "maskE"], ["a_scr"])
                        else:
                            P.op("scalar", lambda h, kb=kb, pb=pb: h.copy(
                                 scr[:, 512 * kb:512 * (kb + 1)], pb),
                                 [pk], ["a_scr"])

                def qsm_head(qi):
                    nk = qi // 4 + 1
                    rl = 512 * nk
                    P.op("vector", lambda h, rl=rl: h.reduce_max(
                         mv[:, 0:1], scr[:, 0:rl], axis=AX.X), ["a_scr"], ["mv"])
                    P.op("vector", lambda h: h.tensor_scalar(mv[:, 1:2],
                         mv[:, 0:1], -CSC, None, op0=OP.mult), ["mv"], ["mv"])
                    P.op("scalar", lambda h, rl=rl: h.activation(
                         scr[:, 0:rl], scr[:, 0:rl], AF.Exp, scale=CSC,
                         bias=mv[:, 1:2], accum_out=mv[:, 0:1]),
                         ["a_scr", "mv"], ["a_scr", "mv"])
                    P.op("vector", lambda h: h.reciprocal(mv[:, 1:2],
                         mv[:, 0:1]), ["mv"], ["mv"])
                    P.op("vector", lambda h, rl=rl: h.tensor_scalar(
                         scr[:, 0:rl], scr[:, 0:rl], mv[:, 1:2], 0.0,
                         op0=OP.mult, op1=OP.max), ["a_scr", "mv"], ["a_scr"])

                def q_av(qi):
                    nk = qi // 4 + 1
                    nch = nk * 4
                    for kt in range(nch):
                        wtbuf, wtk = wtb[kt % 2], f"a_wt{kt % 2}"
                        pb, pk = (p_tr, "pb0") if kt % 2 == 0 else (p_tr2, "pb1")
                        P.op("tensor", lambda h, kt=kt, pb=pb: h.transpose(
                             pb, scr[:, 128 * kt:128 * (kt + 1)], eye),
                             ["a_scr", "eye"], [pk])
                        P.op("scalar", lambda h, pb=pb, wtbuf=wtbuf: h.copy(
                             wtbuf, pb), [pk], [wtk])
                        def mm_av(h, kt=kt, wtbuf=wtbuf, nch=nch):
                            for hh in range(2):
                                r = h.matmul(p_av[:, 512 * hh:512 * (hh + 1)],
                                             wtbuf, V_tm[:, kt, 512 * hh:512 * (hh + 1)],
                                             start=(kt == 0), stop=(kt == nch - 1))
                            return r
                        P.op("tensor", mm_av, [wtk, "a_V"], ["pb4", "pb5"])
                    P.op("scalar", lambda h: h.copy(ytm[0], p_av),
                         ["pb4", "pb5"], ["ytm0"])
                    r0 = 1 + base + 128 * qi
                    dma("gpsimd", "xout", X[r0:r0 + 128, :], ytm[0],
                        ["ytm0"], ["X"], accum=True)

                qprep(0)
                for qi in range(16):
                    qscores(qi)
                    qsm_head(qi)
                    if qi + 1 < 16:
                        qprep(qi + 1)
                    q_av(qi)

                P.fence()

        for sl in SUBLAYERS:
            if sl == "mamba":
                emit_mamba()
            elif sl == "ffn":
                emit_ffn(0)
            elif sl == "attn":
                emit_attn()
        dma("sync", "yout", y_out, X[1:NT + 1, :], ["X"], ["yo"])
        P.op("gpsimd", lambda h: h.engine_nop(), ["yo"], [])

        P.materialise(block, sems, dsems)
    return nc


def _host_constants():
    c = {}
    c["c_eye"] = np.eye(128, dtype=np.float32)
    c["c_ltri"] = np.ascontiguousarray(
        np.tril(np.ones((128, 128), np.float32), -1).T)
    c["c_iota640"] = np.ascontiguousarray(
        np.broadcast_to(np.arange(NSLOT, dtype=np.float32), (128, NSLOT)))
    p = np.arange(128, dtype=np.float32)[:, None]
    yv = np.arange(896, dtype=np.float32)[None, :]
    c["c_maskE"] = np.where(yv <= 384 + p, 0.0, -BIG).astype(np.float32)
    c["c_ids"] = (np.arange(32, dtype=np.float32)[None, :] * 128
                  + np.arange(128, dtype=np.float32)[:, None] + 1.0
                  ).astype(np.float32)
    en = np.zeros((16, 16 * 128), np.float32)
    for n in range(16):
        en[n, 128 * n:128 * (n + 1)] = 1.0
    c["c_en"] = en
    ep = (np.arange(128) // 2) % 8
    esel = np.zeros((128, 8), np.float32)
    esel[np.arange(128), ep] = 1.0
    c["c_esel"] = esel
    c["c_eselT"] = np.ascontiguousarray(esel.T)
    e8 = np.zeros((8, 8 * 128), np.float32)
    for e_ in range(8):
        e8[e_, 128 * e_:128 * (e_ + 1)] = 1.0
    c["c_e8"] = e8
    c["c_zrow"] = np.zeros((1, D), np.float32)
    return c


def kernel(**inputs):
    x0 = np.asarray(inputs["L0_attn_a_u"], np.float32)
    shards = [np.ascontiguousarray(x0[2 * c:2 * c + 2].reshape(NT, D))
              for c in range(NCORES)]
    wmap = dict(
        w_xdtT=np.ascontiguousarray(np.asarray(inputs["L0_attn_a_W_xdt"], np.float32).T),
        w_dtT=np.ascontiguousarray(np.asarray(inputs["L0_attn_a_W_dt"], np.float32).T),
        w_bT=np.ascontiguousarray(np.asarray(inputs["L0_attn_a_W_B"], np.float32).T),
        w_cT=np.ascontiguousarray(np.asarray(inputs["L0_attn_a_W_C"], np.float32).T),
    )
    for k in ["L0_attn_a_A_log", "L0_attn_a_b_xdt", "L0_attn_a_b_dt",
              "L0_attn_a_b_B", "L0_attn_a_b_C", "L0_attn_a_D",
              "L0_attn_b_Wk", "L0_attn_b_Wq", "L0_attn_b_Wv",
              "L0_ffn_Wr", "L0_ffn_br", "L0_ffn_W_up", "L0_ffn_b_up",
              "L0_ffn_W_down", "L0_ffn_b_down"]:
        wmap[k] = np.ascontiguousarray(np.asarray(inputs[k], np.float32))
    b_up = np.asarray(inputs["L0_ffn_b_up"], np.float32)
    bup_pad = np.zeros((4, NFC * 128), np.float32)
    bup_pad[:, :FF] = b_up
    wmap["c_bup_fm"] = np.ascontiguousarray(
        bup_pad.reshape(4, NFC, 128).transpose(2, 0, 1))
    wmap["c_bdt_fm"] = np.ascontiguousarray(
        np.asarray(inputs["L0_attn_a_b_dt"], np.float32).reshape(DC, 128).T)
    wmap["c_d_fm"] = np.ascontiguousarray(
        np.asarray(inputs["L0_attn_a_D"], np.float32).reshape(DC, 128).T)
    wmap.update(_host_constants())
    meta = {k: v.shape for k, v in wmap.items()}
    meta["x0"] = (NT, D)
    nc = build_nc(meta)
    in_maps = []
    for c in range(NCORES):
        m = dict(wmap)
        m["x0"] = shards[c]
        in_maps.append(m)
    res = run_bass_kernel_spmd(nc, in_maps, core_ids=list(range(NCORES)))
    return np.concatenate([r["y"].reshape(2, T, D) for r in res.results], axis=0)


# version 12
# speedup vs baseline: 1.0913x; 1.0913x over previous
import numpy as np
import concourse.bass as bass
import concourse.mybir as mybir
from concourse.bass_utils import run_bass_kernel_spmd
from contextlib import ExitStack

F32 = mybir.dt.float32
I32 = mybir.dt.int32
AF = mybir.ActivationFunctionType
OP = mybir.AluOpType
AX = mybir.AxisListType

NCORES = 8
B_LOC = 2
T = 2048
NT = B_LOC * T
D = 1024
DC = D // 128
N = 16
R = 64
E = 8
FF = 2730
NSLOT = 640
NFC = 22
TC = 256
BIG = 3.0e38

SUBLAYERS = ("mamba", "ffn", "attn", "ffn")


class Prog:
    ENGS = ("tensor", "vector", "scalar", "gpsimd", "sync")

    def __init__(self):
        self.ops = []
        self.last_w = {}
        self.readers = {}
        self.dma_counts = {}
        self.dma_waited = {}
        self.fence_tok = None

    def op(self, eng, fn, reads=(), writes=(), dma=None):
        idx = len(self.ops)
        deps = set()
        dwaits = {}

        def need(t):
            rec = self.ops[t]
            if rec["dma"] is not None:
                s = rec["dma"]
                dwaits[s] = 16 * self.dma_counts[s]
            else:
                deps.add(t)
        if self.fence_tok is not None:
            deps.add(self.fence_tok)
        for k in reads:
            t = self.last_w.get(k)
            if t is not None:
                need(t)
        for k in writes:
            t = self.last_w.get(k)
            if t is not None:
                need(t)
            for t2 in self.readers.get(k, ()):
                need(t2)
        extra_wait = None
        if dma is not None:
            n0 = self.dma_counts.get(dma, 0)
            self.dma_counts[dma] = n0 + 1
            last = self.dma_waited.get(dma, 0)
            if n0 + 1 - last > 8:
                extra_wait = (dma, 16 * n0)
                self.dma_waited[dma] = n0
        if extra_wait is not None:
            dwaits[extra_wait[0]] = max(dwaits.get(extra_wait[0], 0), extra_wait[1])
        self.ops.append(dict(eng=eng, fn=fn, deps=sorted(deps), dma=dma,
                             needed=False, dwaits=dwaits))
        for k in writes:
            self.last_w[k] = idx
            self.readers[k] = []
        for k in reads:
            self.readers.setdefault(k, []).append(idx)
        return idx

    def fence(self):
        keys = list(self.last_w.keys())
        rkeys = [k for k, v in self.readers.items() if v]
        self.fence_tok = self.op("gpsimd", lambda h: h.engine_nop(),
                                 reads=keys, writes=list(set(keys + rkeys)))

    def materialise(self, block, sems, dsems):
        for rec in self.ops:
            for d in rec["deps"]:
                self.ops[d]["needed"] = True
        cnt = {e: 0 for e in self.ENGS}
        for rec in self.ops:
            if rec["dma"] is not None:
                rec["sig"] = ("d", rec["dma"], 16)
            elif rec["needed"]:
                cnt[rec["eng"]] += 1
                rec["sig"] = ("e", rec["eng"], cnt[rec["eng"]])
            else:
                rec["sig"] = None
        marks = {e: {} for e in self.ENGS}
        per_eng = {e: [] for e in self.ENGS}
        for rec in self.ops:
            eng = rec["eng"]
            m = marks[eng]
            waits = []
            for dname, val in sorted(rec["dwaits"].items()):
                key = ("d", dname)
                if m.get(key, 0) >= val:
                    continue
                m[key] = val
                waits.append((dsems[dname], val))
            for d in sorted(rec["deps"]):
                kind, src, val = self.ops[d]["sig"]
                key = (kind, src)
                if m.get(key, 0) >= val:
                    continue
                m[key] = val
                waits.append((sems[src] if kind == "e" else dsems[src], val))
            if rec["sig"] is not None:
                kind, src, val = rec["sig"]
                inc = (sems[src] if kind == "e" else dsems[src],
                       16 if kind == "d" else 1)
            else:
                inc = None
            per_eng[eng].append((waits, rec["fn"], inc))
        for e in self.ENGS:
            def body(h, e=e):
                for waits, fn, inc in per_eng[e]:
                    for sem, val in waits:
                        h.wait_ge(sem, val)
                    inst = fn(h)
                    if inc is not None:
                        inst.then_inc(*inc)
            getattr(block, e)(body)


def build_nc(meta):
    nc = bass.Bass("TRN2", target_bir_lowering=False)
    P = Prog()

    dram = {}
    for name, shape in meta.items():
        dram[name] = nc.dram_tensor(name, list(shape), F32,
                                    kind="ExternalInput").ap()
    y_out = nc.dram_tensor("y", [NT, D], F32, kind="ExternalOutput").ap()
    X = nc.dram_tensor("Xres", [NT + 1, D], F32).ap()
    LNB = nc.dram_tensor("LNbuf", [NT + 1, D], F32).ap()
    UFM2 = nc.dram_tensor("Ufm2", [B_LOC * D, T], F32).ap()
    CCIN = nc.dram_tensor("ccin", [E, NT], F32).ap()
    CCOUT = nc.dram_tensor("ccout", [NCORES * E, NT], F32).ap()

    ctx = ExitStack()

    def alloc(name, shape, dtype=F32):
        return ctx.enter_context(nc.sbuf_tensor("sb_" + name, list(shape), dtype)).ap()

    def palloc(name, shape):
        return ctx.enter_context(nc.psum_tensor(name, list(shape), F32)).ap()

    with ctx:
        eye = alloc("eye", [128, 128])
        ltri = alloc("ltri", [128, 128])
        ones1 = alloc("ones1", [1, 128])
        maskE = alloc("maskE", [128, 896])
        ids32 = alloc("ids32", [128, 32])
        e_n = alloc("e_n", [16, N * 128])
        esel = alloc("esel", [128, 8])
        eselT = alloc("eselT", [8, 128])
        e8 = alloc("e8", [8, 8 * 128])
        w_xdtT = alloc("w_xdtT", [128, DC, R])
        w_bT = alloc("w_bT", [128, DC, N])
        w_cT = alloc("w_cT", [128, DC, N])
        s_A = alloc("s_A", [128, DC, N])
        b_xdt = alloc("b_xdt", [R, 1])
        b_dt = alloc("b_dt", [128, DC])
        b_B = alloc("b_B", [N, 1])
        b_C = alloc("b_C", [N, 1])
        s_D = alloc("s_D", [128, DC])
        wr = alloc("wr", [128, DC, E])
        br = alloc("br", [1, E])
        bup = alloc("bup", [128, 4, NFC])
        lnm = alloc("lnm", [128, 32])
        lnr = alloc("lnr", [128, 32])
        stat = alloc("stat", [128, 12])
        mv = alloc("mv", [128, 2])
        tiny = alloc("tiny", [128, 16])
        ones32 = alloc("ones32", [128, 32])
        s_sm_st = alloc("s_sm_st", [128, 32, E])
        alloc_gidx = [alloc("gidx_i", [128, 5], I32), alloc("gv_f", [128, 5])]
        xt = [alloc(f"xt{i}", [128, D]) for i in range(2)]
        ut = [alloc(f"ut{i}", [128, D]) for i in range(2)]
        ytm = [alloc(f"ytm{i}", [128, D]) for i in range(2)]
        A = alloc("arenaA", [128, 16384])
        Bb = alloc("arenaB", [128, 16384])
        Cc = alloc("arenaC", [128, 8192])
        w_dtT = Cc[0:R, 6144:7168]

        pA = palloc("pA", [128, 4096])
        p_tr = pA[:, 0:128]
        p_tr2 = pA[:, 512:640]
        p_a = pA[:, 1024:1536]
        p_b = pA[:, 1536:2048]
        p_c = pA[:, 2048:2560]
        p_d = pA[:, 2560:3072]
        p_e = pA[:, 3072:3584]
        p_f = pA[:, 3584:4096]
        p_mid = pA[:, 3072:3712]
        p_dn = pA[:, 1024:3584].rearrange("p (s h) -> p s h", s=5)

        sems = {e: ctx.enter_context(nc.semaphore(f"ps_{e}")) for e in Prog.ENGS}
        dnames = ["wload", "xin", "xrd", "xrd2", "xout", "wu0", "wu1",
                  "wd0", "wd1", "wbd", "lnw", "gat", "scat", "ccio", "yout",
                  "xs0", "xs1", "misc", "ufw"]
        dsems = {n: ctx.enter_context(nc.semaphore(f"ds_{n}")) for n in dnames}
        ccsem = ctx.enter_context(nc.semaphore("ccsem"))

        block = ctx.enter_context(nc.Block())

        def dma(qeng, sem, out, in_, reads, writes, accum=False):
            def fn(h, out=out, in_=in_, accum=accum):
                if accum:
                    return h.dma_start(out=out, in_=in_, accum_op=OP.add)
                return h.dma_start(out=out, in_=in_)
            return P.op(qeng, fn, reads=reads, writes=writes, dma=sem)

        for dst, srcname, key in [(eye, "c_eye", "eye"), (ltri, "c_ltri", "ltri"),
                             (maskE, "c_maskE", "maskE"),
                             (ids32, "c_ids", "ids32"), (e_n, "c_en", "e_n"),
                             (esel, "c_esel", "esel"), (eselT, "c_eselT", "eselT"),
                             (e8, "c_e8", "e8")]:
            dma("sync", "wload", dst, dram[srcname], [], [key])
        P.op("vector", lambda h: h.memset(ones1, 1.0), [], ["ones1"])
        P.op("vector", lambda h: h.memset(ones32, 1.0), [], ["ones32"])
        dma("sync", "wload", w_xdtT,
            dram["w_xdtT"].rearrange("(c p) r -> p c r", p=128), [], ["w_xdtT"])
        dma("sync", "wload", w_dtT, dram["w_dtT"], [], ["m_wdtT"])
        dma("sync", "wload", w_bT,
            dram["w_bT"].rearrange("(c p) n -> p c n", p=128), [], ["w_bT"])
        dma("sync", "wload", w_cT,
            dram["w_cT"].rearrange("(c p) n -> p c n", p=128), [], ["w_cT"])
        dma("sync", "wload", s_A,
            dram["L0_attn_a_A_log"].rearrange("(c p) n -> p c n", p=128),
            [], ["s_A"])
        dma("sync", "wload", b_xdt, dram["L0_attn_a_b_xdt"].unsqueeze(1),
            [], ["b_xdt"])
        dma("sync", "wload", b_dt, dram["c_bdt_fm"], [], ["b_dt"])
        dma("sync", "wload", b_B, dram["L0_attn_a_b_B"].unsqueeze(1), [], ["b_B"])
        dma("sync", "wload", b_C, dram["L0_attn_a_b_C"].unsqueeze(1), [], ["b_C"])
        dma("sync", "wload", s_D, dram["c_d_fm"], [], ["s_D"])
        dma("sync", "wload", wr,
            dram["L0_ffn_Wr"].rearrange("(c p) e -> p c e", p=128), [], ["wr"])
        dma("sync", "wload", br, dram["L0_ffn_br"].unsqueeze(0), [], ["br"])
        dma("sync", "wload", bup, dram["c_bup_fm"], [], ["bup"])
        P.op("scalar", lambda h: h.activation(s_A, s_A, AF.Exp), ["s_A"], ["s_A"])
        P.op("vector", lambda h: h.tensor_scalar(s_A, s_A, -1.0, None,
                                                 op0=OP.mult), ["s_A"], ["s_A"])
        dma("sync", "xin", X[0:1, :], dram["c_zrow"], [], ["X"])
        dma("sync", "xin", X[1:NT + 1, :], dram["x0"], [], ["X"])
        dma("sync", "xin", LNB[0:1, :], dram["c_zrow"], [], ["LNB"])

        def ln_stats_all():
            for c in range(32):
                slot = c % 2
                xk = f"xt{slot}"
                dma("sync", f"xs{slot}", xt[slot],
                    X[1 + 128 * c:1 + 128 * (c + 1), :], ["X"], [xk])
                P.op("vector", lambda h, s=slot: h.bn_stats(stat[:, 0:6],
                     xt[s][:, 0:512]), [xk], ["stat"])
                P.op("vector", lambda h, s=slot: h.bn_stats(stat[:, 6:12],
                     xt[s][:, 512:1024]), [xk], ["stat"])
                P.op("vector", lambda h: h.bn_aggr(mv, stat), ["stat"], ["mv"])
                P.op("vector", lambda h, c=c: h.tensor_scalar(
                    lnr[:, c:c + 1], mv[:, 1:2], 1e-5, None, op0=OP.add),
                    ["mv"], ["lnr"])
                P.op("vector", lambda h, c=c: h.reciprocal(
                    lnr[:, c:c + 1], lnr[:, c:c + 1]), ["lnr"], ["lnr"])
                P.op("scalar", lambda h, c=c: h.activation(
                    lnr[:, c:c + 1], lnr[:, c:c + 1], AF.Sqrt), ["lnr"], ["lnr"])
                P.op("vector", lambda h, c=c: h.tensor_tensor(
                    lnm[:, c:c + 1], mv[:, 0:1], lnr[:, c:c + 1], op=OP.mult),
                    ["mv", "lnr"], ["lnm"])
                P.op("vector", lambda h, c=c: h.tensor_scalar(
                    lnm[:, c:c + 1], lnm[:, c:c + 1], -1.0, None, op0=OP.mult),
                    ["lnm"], ["lnm"])

        def u_token_tile(c, slot):
            xk = f"xt{slot}"
            uk = f"ut{slot}"
            dma("sync", f"xs{slot}", xt[slot],
                X[1 + 128 * c:1 + 128 * (c + 1), :], ["X"], [xk])
            P.op("scalar", lambda h, c=c, s=slot: h.activation(
                ut[s], xt[s], AF.Identity, bias=lnm[:, c:c + 1],
                scale=lnr[:, c:c + 1]), [xk, "lnm", "lnr"], [uk])
            return uk

        def transpose_to(src_ap, src_key, dst_ap, dst_key, pi):
            pb, pk = (p_tr, "pb0") if pi % 2 == 0 else (p_tr2, "pb1")
            P.op("tensor", lambda h: h.transpose(pb, src_ap, eye),
                 [src_key, "eye"], [pk])
            P.op("scalar", lambda h: h.copy(dst_ap, pb), [pk], [dst_key])

        def emit_mamba():
            ln_stats_all()
            for s in range(B_LOC):
                base = s * T
                ufm = A[:, 0:DC * 512].rearrange("p (c t) -> p c t", c=DC)
                xdt = Cc[:, 0:2048]
                bt = Cc[:, 2048:4096]
                ct = Cc[:, 4096:6144]
                for blk in range(T // 512):
                    t0 = 512 * blk
                    for tb in range(4):
                        c = (base + t0) // 128 + tb
                        slot = tb % 2
                        uk = u_token_tile(c, slot)
                        for d in range(DC):
                            transpose_to(ut[slot][:, 128 * d:128 * (d + 1)], uk,
                                         ufm[:, d, 128 * tb:128 * (tb + 1)],
                                         "mA_ufm", tb * DC + d)
                    for d in range(DC):
                        dma("sync", "ufw",
                            UFM2[(s * DC + d) * 128:(s * DC + d + 1) * 128,
                                 t0:t0 + 512],
                            ufm[:, d, :], ["mA_ufm"], ["mA_UFM"])
                    def mk_mm(psum, wap, wkey, nrows):
                        def f(h):
                            for d in range(DC):
                                r = h.matmul(psum, wap[:, d, :],
                                             ufm[:, d, 0:512],
                                             start=(d == 0), stop=(d == DC - 1))
                            return r
                        return f
                    P.op("tensor", mk_mm(p_a[0:R, :], w_xdtT, "w_xdtT", R),
                         ["mA_ufm", "w_xdtT"], ["p_a"])
                    P.op("scalar", lambda h, t0=t0: h.activation(
                        xdt[0:R, t0:t0 + 512], p_a[0:R, :], AF.Identity,
                        bias=b_xdt), ["p_a", "b_xdt"], ["m_xdt"])
                    P.op("tensor", mk_mm(p_b[0:N, :], w_bT, "w_bT", N),
                         ["mA_ufm", "w_bT"], ["p_b"])
                    P.op("scalar", lambda h, t0=t0: h.activation(
                        bt[0:N, t0:t0 + 512], p_b[0:N, :], AF.Identity,
                        bias=b_B), ["p_b", "b_B"], ["m_bt"])
                    P.op("tensor", mk_mm(p_c[0:N, :], w_cT, "w_cT", N),
                         ["mA_ufm", "w_cT"], ["p_c"])
                    P.op("scalar", lambda h, t0=t0: h.activation(
                        ct[0:N, t0:t0 + 512], p_c[0:N, :], AF.Identity,
                        bias=b_C), ["p_c", "b_C"], ["m_ct"])
                P.fence()
                ab = [A[:, 0:4096].rearrange("p (n t) -> p n t", n=N),
                      A[:, 4096:8192].rearrange("p (n t) -> p n t", n=N)]
                xs3 = A[:, 8192:12288].rearrange("p (n t) -> p n t", n=N)
                h3 = A[:, 12288:16384].rearrange("p (n t) -> p n t", n=N)
                w3o = A[:, 8192:12288].rearrange("p (t n) -> p n t", n=N)
                w3r = A[:, 8192:12288].rearrange("p (t n) -> p t n", n=N)
                bbcs = [Bb[:, 0:4096].rearrange("p (n t) -> p n t", n=N),
                        Bb[:, 8192:12288].rearrange("p (n t) -> p n t", n=N)]
                cbcs = [Bb[:, 4096:8192].rearrange("p (n t) -> p n t", n=N),
                        Bb[:, 12288:16384].rearrange("p (n t) -> p n t", n=N)]
                udcs = [Cc[:, 7168:7424], ut[1][:, 256:512]]
                dtu = Cc[:, 7424:7680]
                dts = [Cc[:, 7680:7936], Cc[:, 7936:8192]]
                red = ut[0][:, 256:512]
                yfm = ut[0][:, 512:768]
                hc = ut[0][:, 768:768 + DC * N].rearrange("p (c n) -> p c n", c=DC)
                P.op("vector", lambda h: h.memset(
                    ut[0][:, 768:768 + DC * N], 0.0), [], ["m_hc"])
                NG = (T // TC) * DC

                def prep(gi):
                    tci, d = divmod(gi, DC)
                    t0 = tci * TC
                    par = tci % 2
                    g = gi % 2
                    dt_, dtk = dts[g], f"m_dt{g}"
                    abg, abk = ab[g], f"m_ab{g}"
                    udc, udck = udcs[g], f"m_udc{g}"
                    if d == 0:
                        bbc, cbc = bbcs[par], cbcs[par]
                        bbck, cbck = f"m_bbc{par}", f"m_cbc{par}"
                        for srck, src_, dst, dstk in (("m_bt", bt, bbc, bbck),
                                                      ("m_ct", ct, cbc, cbck)):
                            for n in range(N):
                                pb, pk = (p_c, "pb4") if n % 2 == 0 else (p_d, "pb5")
                                P.op("tensor", lambda h, n=n, src_=src_, t0=t0, pb=pb:
                                     h.matmul(pb[:, 0:TC],
                                              e_n[:, 128 * n:128 * (n + 1)],
                                              src_[0:N, t0:t0 + TC],
                                              start=True, stop=True),
                                     [srck, "e_n"], [pk])
                                P.op("scalar", lambda h, n=n, dst=dst, pb=pb:
                                     h.copy(dst[:, n, :], pb[:, 0:TC]),
                                     [pk], [dstk])
                    dma("sync", "xrd" if gi % 2 == 0 else "xrd2", udc,
                        UFM2[(s * DC + d) * 128:(s * DC + d + 1) * 128,
                             t0:t0 + TC], ["mA_UFM"], [udck])
                    pdt, pdtk = (p_a, "pb2") if g == 0 else (p_b, "pb3")
                    P.op("tensor", lambda h, d=d, t0=t0, pdt=pdt: h.matmul(
                        pdt[:, 0:TC], w_dtT[:, 128 * d:128 * (d + 1)],
                        xdt[0:R, t0:t0 + TC], start=True, stop=True),
                        ["m_xdt", "m_wdtT"], [pdtk])
                    P.op("scalar", lambda h, d=d, dt_=dt_, pdt=pdt: h.activation(
                        dt_, pdt[:, 0:TC], AF.Exp, bias=b_dt[:, d:d + 1]),
                        [pdtk, "b_dt"], [dtk])
                    P.op("scalar", lambda h, dt_=dt_: h.activation(
                        dt_, dt_, AF.Ln, bias=1.0), [dtk], [dtk])
                    for n in range(N):
                        P.op("scalar", lambda h, d=d, n=n, dt_=dt_, abg=abg:
                             h.activation(abg[:, n, :], dt_, AF.Exp,
                                          scale=s_A[:, d, n:n + 1]),
                             [dtk, "s_A"], [abk])

                def compute(gi):
                    tci, d = divmod(gi, DC)
                    t0 = tci * TC
                    par = tci % 2
                    g = gi % 2
                    dt_, dtk = dts[g], f"m_dt{g}"
                    abg, abk = ab[g], f"m_ab{g}"
                    udc, udck = udcs[g], f"m_udc{g}"
                    bbc, cbc = bbcs[par], cbcs[par]
                    bbck, cbck = f"m_bbc{par}", f"m_cbc{par}"
                    P.op("vector", lambda h, dt_=dt_, udc=udc: h.tensor_tensor(
                        dtu, dt_, udc, op=OP.mult), [dtk, udck], ["m_dtu"])
                    P.op("vector", lambda h, bbc=bbc: h.tensor_tensor(
                        xs3, bbc, dtu.unsqueeze(1).to_broadcast([128, N, TC]),
                        op=OP.mult), [bbck, "m_dtu"], ["m_xs"])
                    P.op("vector", lambda h, d=d, abg=abg: h.tensor_tensor(
                        tiny, abg[:, :, 0], hc[:, d, :], op=OP.mult),
                        [abk, "m_hc"], ["tiny"])
                    P.op("vector", lambda h: h.tensor_tensor(
                        xs3[:, :, 0], xs3[:, :, 0], tiny, op=OP.add),
                        ["tiny", "m_xs"], ["m_xs"])
                    P.op("vector", lambda h, abg=abg: h.memset(
                        abg[:, :, 0], 0.0), [abk], [abk])
                    P.op("vector", lambda h, abg=abg: h.tensor_tensor_scan(
                        h3.rearrange("p n t -> p (n t)"),
                        abg.rearrange("p n t -> p (n t)"),
                        xs3.rearrange("p n t -> p (n t)"),
                        0.0, op0=OP.mult, op1=OP.add),
                        [abk, "m_xs"], ["m_h"])
                    P.op("vector", lambda h, d=d: h.tensor_copy(
                        hc[:, d, :], h3[:, :, TC - 1]), ["m_h"], ["m_hc"])
                    P.op("vector", lambda h, cbc=cbc: h.tensor_tensor(
                        w3o, h3, cbc, op=OP.mult), ["m_h", cbck], ["m_xs"])
                    P.op("vector", lambda h: h.reduce_sum(
                        red, w3r, axis=AX.X), ["m_xs"], ["m_red"])
                    P.op("vector", lambda h, d=d, udc=udc: h.scalar_tensor_tensor(
                        yfm, udc, s_D[:, d:d + 1], red, op0=OP.mult,
                        op1=OP.add), [udck, "s_D", "m_red"], ["m_yfm"])
                    for j in range(TC // 128):
                        transpose_to(yfm[:, 128 * j:128 * (j + 1)], "m_yfm",
                                     ytm[j][:, 128 * d:128 * (d + 1)],
                                     f"ytm{j}", d * 2 + j)
                    if d == DC - 1:
                        for j in range(TC // 128):
                            r0 = 1 + base + t0 + 128 * j
                            dma("gpsimd", "xout", X[r0:r0 + 128, :], ytm[j],
                                [f"ytm{j}"], ["X"], accum=True)

                prep(0)
                for gi in range(NG):
                    if gi + 1 < NG:
                        prep(gi + 1)
                    compute(gi)
                P.fence()

            P.fence()

        FCS = [128] * 21 + [42]

        def emit_ffn(call):
            ln_stats_all()
            ufm = A[:, 0:1024].rearrange("p (d t) -> p d t", d=DC)
            smT = Cc[0:8, 0:4096]
            s_sm = s_sm_st
            for c in range(32):
                slot = c % 2
                uk = u_token_tile(c, slot)
                dma("gpsimd", "lnw", LNB[1 + 128 * c:1 + 128 * (c + 1), :],
                    ut[slot], [uk], ["LNB"])
                for d in range(DC):
                    transpose_to(ut[slot][:, 128 * d:128 * (d + 1)], uk,
                                 ufm[:, d, :], "f_ufm", d)
                def mm_rt(h):
                    for d in range(DC):
                        h.matmul(p_a[:, 0:E], ufm[:, d, :], wr[:, d, :],
                                 start=(d == 0), stop=False)
                    return h.matmul(p_a[:, 0:E], ones1, br, start=False,
                                    stop=True)
                P.op("tensor", mm_rt, ["f_ufm", "wr", "br", "ones1"], ["pb2"])
                P.op("vector", lambda h: h.reduce_max(mv[:, 0:1], p_a[:, 0:E],
                     axis=AX.X), ["pb2"], ["mv"])
                P.op("vector", lambda h: h.tensor_scalar(mv[:, 1:2], mv[:, 0:1],
                     -1.0, None, op0=OP.mult), ["mv"], ["mv"])
                P.op("scalar", lambda h: h.activation(stat[:, 0:E], p_a[:, 0:E],
                     AF.Exp, bias=mv[:, 1:2], accum_out=mv[:, 0:1]),
                     ["pb2", "mv"], ["stat", "mv"])
                P.op("vector", lambda h: h.reciprocal(mv[:, 1:2], mv[:, 0:1]),
                     ["mv"], ["mv"])
                P.op("vector", lambda h, c=c: h.tensor_scalar(s_sm[:, c, :],
                     stat[:, 0:E], mv[:, 1:2], None, op0=OP.mult),
                     ["stat", "mv"], ["f_sm"])
                P.op("tensor", lambda h, c=c: h.transpose(p_tr2[0:E, :],
                     s_sm[:, c, :], eye), ["f_sm", "eye"], ["pb1"])
                P.op("scalar", lambda h, c=c: h.copy(smT[:, 128 * c:128 * (c + 1)],
                     p_tr2[0:E, :]), ["pb1"], ["f_smT"])
            dma("gpsimd", "ccio", CCIN, smT, ["f_smT"], ["CCIN"])
            def cc(h):
                return h.collective_compute(
                    "AllGather", OP.bypass,
                    replica_groups=[list(range(NCORES))],
                    ins=[CCIN], outs=[CCOUT])
            P.op("gpsimd", cc, ["CCIN"], ["CCOUT"])
            smA = A[:, 4096:6144]
            dma("gpsimd", "ccio", smA,
                CCOUT.rearrange("(r e) (hh t) -> (r e hh) t", r=NCORES, hh=2),
                ["CCOUT"], ["f_smA"])
            bis = Cc[0:8, 4096:4104]
            lo8, hi8, tau8 = bis[:, 0:1], bis[:, 1:2], bis[:, 2:3]
            cnt8, m8, d18 = bis[:, 3:4], bis[:, 4:5], bis[:, 5:6]
            c128 = mv[:, 0:1]
            P.op("vector", lambda h: h.memset(lo8, 0.0), [], ["f_bis"])
            P.op("vector", lambda h: h.memset(hi8, 1.0), [], ["f_bis"])
            P.op("vector", lambda h: h.memset(tau8, 0.5), [], ["f_bis"])
            fcmp = Bb[:, 0:2048]
            for it in range(30):
                P.op("tensor", lambda h: h.matmul(p_e[:, 0:1], eselT, tau8,
                     start=True, stop=True), ["f_bis", "eselT"], ["pb6"])
                P.op("vector", lambda h: h.tensor_scalar(fcmp, smA, p_e[:, 0:1],
                     0.0, op0=OP.is_gt, op1=OP.add, accum_out=c128),
                     ["f_smA", "pb6"], ["f_cmp", "mv"])
                P.op("tensor", lambda h: h.matmul(p_f[0:8, 0:1], esel, c128,
                     start=True, stop=True), ["mv", "esel"], ["pb7"])
                P.op("vector", lambda h: h.tensor_copy(cnt8, p_f[0:8, 0:1]),
                     ["pb7"], ["f_bis"])
                P.op("vector", lambda h: h.tensor_scalar(m8, cnt8,
                     float(NT), None, op0=OP.is_ge), ["f_bis"], ["f_bis"])
                P.op("vector", lambda h: h.tensor_tensor(d18, tau8, lo8,
                     op=OP.subtract), ["f_bis"], ["f_bis"])
                P.op("vector", lambda h: h.tensor_tensor(d18, d18, m8,
                     op=OP.mult), ["f_bis"], ["f_bis"])
                P.op("vector", lambda h: h.tensor_tensor(lo8, lo8, d18,
                     op=OP.add), ["f_bis"], ["f_bis"])
                P.op("vector", lambda h: h.tensor_tensor(d18, tau8, hi8,
                     op=OP.subtract), ["f_bis"], ["f_bis"])
                P.op("vector", lambda h: h.tensor_tensor(d18, d18, m8,
                     op=OP.mult), ["f_bis"], ["f_bis"])
                P.op("vector", lambda h: h.tensor_tensor(hi8, tau8, d18,
                     op=OP.subtract), ["f_bis"], ["f_bis"])
                P.op("vector", lambda h: h.tensor_tensor(tau8, lo8, hi8,
                     op=OP.add), ["f_bis"], ["f_bis"])
                P.op("vector", lambda h: h.tensor_scalar(tau8, tau8, 0.5,
                     None, op0=OP.mult), ["f_bis"], ["f_bis"])
            P.fence()
            iota = Bb[:, 15360:16000]
            dma("sync", "wload", iota, dram["c_iota640"], [], ["f_iota"])
            ug = Bb[:, 0:5120].rearrange("p (j d) -> p j d", j=5)
            ugfm = Bb[:, 5120:10240].rearrange("p (d t) -> p d t", d=DC)
            ys = Bb[:, 10240:15360].rearrange("p (j d) -> p j d", j=5)
            rhs2 = Bb[:, 16064:16128].rearrange("p (c k) -> p c k", c=32)
            mid = A[:, 0:NFC * 640].rearrange("p (f t) -> p f t", f=NFC)
            ohb = [A[:, 14080:14720], A[:, 14720:15360]]
            wub = [Cc[:, 0:1024].rearrange("p (d f) -> p d f", d=DC),
                   Cc[:, 1024:2048].rearrange("p (d f) -> p d f", d=DC)]
            wdb = [Cc[:, 2048:2560], Cc[:, 2560:3072]]
            bdn1 = Cc[0:1, 3072:4096]
            s_msk = Bb[:, 16000:16032]
            s_pfx = Bb[:, 16032:16064]
            s_tot = tiny[:, 0:1]
            s_gidx = alloc_gidx[0]
            s_gv = alloc_gidx[1]
            for e in range(E):
                g = e // 2
                P.op("tensor", lambda h, e=e: h.matmul(p_e[:, 0:1],
                     e8[:, 128 * e:128 * (e + 1)], tau8, start=True, stop=True),
                     ["f_bis", "e8"], ["pb6"])
                P.op("vector", lambda h, e=e: h.tensor_scalar(s_msk,
                     s_sm[:, :, e], p_e[:, 0:1], None, op0=OP.is_gt),
                     ["f_sm", "pb6"], ["f_msk"])
                P.op("vector", lambda h: h.tensor_tensor_scan(s_pfx, ones32,
                     s_msk, 0.0, op0=OP.mult, op1=OP.add),
                     ["f_msk", "ones32"], ["f_pfx"])
                P.op("vector", lambda h: h.reduce_sum(s_tot, s_msk, axis=AX.X),
                     ["f_msk"], ["tiny"])
                P.op("tensor", lambda h: h.matmul(p_f[:, 0:1], ltri, s_tot,
                     start=True, stop=True), ["tiny", "ltri"], ["pb7"])
                P.op("vector", lambda h: h.tensor_scalar(s_pfx, s_pfx,
                     p_f[:, 0:1], -1.0, op0=OP.add, op1=OP.add),
                     ["f_pfx", "pb7"], ["f_pfx"])
                P.op("vector", lambda h: h.scalar_tensor_tensor(s_pfx, s_pfx,
                     1.0, s_msk, op0=OP.add, op1=OP.mult),
                     ["f_pfx", "f_msk"], ["f_pfx"])
                P.op("vector", lambda h: h.tensor_scalar(s_pfx, s_pfx, -1.0,
                     None, op0=OP.add), ["f_pfx"], ["f_pfx"])
                P.op("vector", lambda h: h.tensor_copy(rhs2[:, :, 0:1],
                     ids32.unsqueeze(2)), ["ids32"], ["f_rhs2"])
                P.op("vector", lambda h, e=e: h.tensor_copy(rhs2[:, :, 1:2],
                     s_sm[:, :, e].unsqueeze(2)), ["f_sm"], ["f_rhs2"])
                for c in range(32):
                    ob = ohb[c % 2]
                    obk = f"f_oh{c % 2}"
                    P.op("vector", lambda h, c=c, ob=ob: h.tensor_scalar(
                         ob, iota, s_pfx[:, c:c + 1], None, op0=OP.is_equal),
                         ["f_iota", "f_pfx"], [obk])
                    def mm_g(h, c=c, ob=ob):
                        for sbk in range(5):
                            r = h.matmul(p_a[:, 2 * sbk:2 * sbk + 2],
                                         ob[:, 128 * sbk:128 * (sbk + 1)],
                                         rhs2[:, c, :],
                                         start=(c == 0 and sbk == 0),
                                         stop=(c == 31 and sbk == 4))
                        return r
                    P.op("tensor", mm_g, [obk, "f_rhs2"], ["pb2"])
                P.op("vector", lambda h: h.tensor_copy(
                     s_gidx, p_a[:, 0:10].rearrange("p (s k) -> p s k", k=2)[:, :, 0]),
                     ["pb2"], ["f_gidx"])
                P.op("vector", lambda h: h.tensor_copy(
                     s_gv, p_a[:, 0:10].rearrange("p (s k) -> p s k", k=2)[:, :, 1]),
                     ["pb2"], ["f_gv"])
                for j in range(5):
                    def gath(h, j=j):
                        return h.indirect_dma_start(
                            out=ug[:, j, :], out_offset=None, in_=LNB,
                            in_offset=bass.IndirectOffsetOnAxis(
                                ap=s_gidx[:, j:j + 1], axis=0))
                    P.op("gpsimd", gath, ["LNB", "f_gidx"], ["f_ug"], dma="gat")
                for j in range(5):
                    for d in range(DC):
                        transpose_to(ug[:, j, 128 * d:128 * (d + 1)], "f_ug",
                                     ugfm[:, d, 128 * j:128 * (j + 1)],
                                     "f_ugfm", j * DC + d)
                p_mid2 = pA[:, 2048:2688]
                for fc in range(NFC):
                    wb = wub[fc % 2]
                    wbk = f"f_wu{fc % 2}"
                    fcs = FCS[fc]
                    pm = p_mid if fc % 2 == 0 else p_mid2
                    pmk = ["pb6", "pb7"] if fc % 2 == 0 else ["pb4", "pb5"]
                    dma("sync", f"wu{fc % 2}", wb[:, :, 0:fcs],
                        dram["L0_ffn_W_up"][g, :, 128 * fc:128 * fc + fcs]
                        .rearrange("(d p) f -> p d f", p=128), [], [wbk])
                    def mm_up(h, wb=wb, fcs=fcs, pm=pm, m0=0, m1=512):
                        for d in range(DC):
                            r = h.matmul(pm[0:fcs, m0:m1],
                                         wb[:, d, 0:fcs], ugfm[:, d, m0:m1],
                                         start=(d == 0), stop=(d == DC - 1))
                        return r
                    P.op("tensor", lambda h, f=mm_up: f(h),
                         [wbk, "f_ugfm"], [pmk[0]])
                    P.op("tensor", lambda h, wb=wb, fcs=fcs, pm=pm: mm_up(
                         h, wb=wb, fcs=fcs, pm=pm, m0=512, m1=640),
                         [wbk, "f_ugfm"], [pmk[1]])
                    P.op("scalar", lambda h, fc=fc, fcs=fcs, g=g, pm=pm: h.activation(
                         mid[0:fcs, fc, :], pm[0:fcs, :], AF.Silu,
                         bias=bup[0:fcs, g, fc:fc + 1]),
                         pmk + ["bup"], ["f_mid"])
                dma("sync", "wbd", bdn1,
                    dram["L0_ffn_b_down"][e:e + 1, :], [], ["f_bdn"])
                for hh in range(2):
                    for fc in range(NFC):
                        wb = wdb[fc % 2]
                        wbk = f"f_wd{fc % 2}"
                        fcs = FCS[fc]
                        dma("sync", f"wd{fc % 2}", wb[0:fcs, :],
                            dram["L0_ffn_W_down"][e, 128 * fc:128 * fc + fcs,
                                                  512 * hh:512 * (hh + 1)],
                            [], [wbk])
                        def mm_dn(h, wb=wb, fcs=fcs, fc=fc):
                            for ts in range(5):
                                r = h.matmul(p_dn[:, ts, :],
                                             mid[0:fcs, fc, 128 * ts:128 * (ts + 1)],
                                             wb[0:fcs, :],
                                             start=(fc == 0), stop=False)
                            return r
                        P.op("tensor", mm_dn, [wbk, "f_mid"],
                             [f"pb{2 + ts2}" for ts2 in range(5)])
                    def mm_bias(h, hh=hh):
                        for ts in range(5):
                            r = h.matmul(p_dn[:, ts, :], ones1,
                                         bdn1[:, 512 * hh:512 * (hh + 1)],
                                         start=False, stop=True)
                        return r
                    P.op("tensor", mm_bias, ["f_bdn", "ones1"],
                         [f"pb{2 + ts2}" for ts2 in range(5)])
                    for ts in range(5):
                        P.op("scalar", lambda h, ts=ts, hh=hh: h.activation(
                             ys[:, ts, 512 * hh:512 * (hh + 1)], p_dn[:, ts, :],
                             AF.Identity, scale=s_gv[:, ts:ts + 1]),
                             [f"pb{2 + ts}", "f_gv"], ["f_ys"])
                for j in range(5):
                    def scat(h, j=j):
                        return h.indirect_dma_start(
                            out=X, out_offset=bass.IndirectOffsetOnAxis(
                                ap=s_gidx[:, j:j + 1], axis=0),
                            in_=ys[:, j, :], in_offset=None,
                            compute_op=OP.add)
                    P.op("gpsimd", scat, ["f_ys", "f_gidx"], ["X"], dma="scat")
            P.fence()

        def emit_attn():
            ln_stats_all()
            K_fm = A.rearrange("p (hc t) -> p hc t", hc=DC)
            V_tm = Bb.rearrange("p (tt h) -> p tt h", tt=16)
            scr = Cc[:, 0:2048]
            wtb = [Cc[:, 2048:2176], Cc[:, 2176:2304]]
            q_fm = Cc[:, 3072:4096].rearrange("p (d t) -> p d t", d=DC)
            ufm_q = Cc[:, 4096:5120].rearrange("p (d t) -> p d t", d=DC)
            ufm_b = Cc[:, 4096:8192].rearrange("p (d t) -> p d t", d=DC)
            wbuf1 = [Cc[:, 2048:3072].rearrange("p (d h) -> p d h", d=DC),
                     Cc[:, 3072:4096].rearrange("p (d h) -> p d h", d=DC)]
            wbuf2 = [Cc[:, 5120:6144].rearrange("p (d h) -> p d h", d=DC),
                     Cc[:, 6144:7168].rearrange("p (d h) -> p d h", d=DC)]
            p_av = pA[:, 2048:3072]
            CSC = 1.0 / 32.0
            for s in range(B_LOC):
                base = s * T
                for kb in range(4):
                    t0 = 512 * kb
                    for tb in range(4):
                        c = (base + t0) // 128 + tb
                        slot = tb % 2
                        uk = u_token_tile(c, slot)
                        for d in range(DC):
                            transpose_to(ut[slot][:, 128 * d:128 * (d + 1)], uk,
                                         ufm_b[:, d, 128 * tb:128 * (tb + 1)],
                                         "a_ufb", tb * DC + d)
                    for hc in range(DC):
                        wb, wbk = wbuf1[hc % 2], f"a_w{hc % 2}"
                        dma("sync", f"wu{hc % 2}", wb,
                            dram["L0_attn_b_Wk"][:, 128 * hc:128 * (hc + 1)]
                            .rearrange("(d p) h -> p d h", p=128), [], [wbk])
                        pb, pk = (p_a, "pb2") if hc % 2 == 0 else (p_b, "pb3")
                        def mm_k(h, wb=wb, pb=pb):
                            for d in range(DC):
                                r = h.matmul(pb, wb[:, d, :], ufm_b[:, d, :],
                                             start=(d == 0), stop=(d == DC - 1))
                            return r
                        P.op("tensor", mm_k, [wbk, "a_ufb"], [pk])
                        P.op("scalar", lambda h, hc=hc, t0=t0, pb=pb: h.copy(
                             K_fm[:, hc, t0:t0 + 512], pb), [pk], ["a_K"])
                    for hh in range(2):
                        for d in range(DC):
                            wb, wbk = wbuf1[d % 2], f"a_w{d % 2}"
                            dma("sync", f"wu{d % 2}",
                                wb.rearrange("p d h -> p (d h)")[:, 0:512],
                                dram["L0_attn_b_Wv"][128 * d:128 * (d + 1),
                                                     512 * hh:512 * (hh + 1)],
                                [], [wbk])
                            def mm_v(h, wb=wb, d=d):
                                for tt in range(4):
                                    r = h.matmul(
                                        p_dn[:, tt, :],
                                        ufm_b[:, d, 128 * tt:128 * (tt + 1)],
                                        wb.rearrange("p d h -> p (d h)")[:, 0:512],
                                        start=(d == 0), stop=(d == DC - 1))
                                return r
                            P.op("tensor", mm_v, [wbk, "a_ufb"],
                                 [f"pb{2 + tt2}" for tt2 in range(4)])
                        for tt in range(4):
                            P.op("scalar", lambda h, kb=kb, tt=tt, hh=hh: h.copy(
                                 V_tm[:, 4 * kb + tt, 512 * hh:512 * (hh + 1)],
                                 p_dn[:, tt, :]),
                                 [f"pb{2 + tt}"], ["a_V"])
                P.fence()
                def qprep(qi):
                    cq = (base + 128 * qi) // 128
                    slot = qi % 2
                    uk = u_token_tile(cq, slot)
                    for d in range(DC):
                        transpose_to(ut[slot][:, 128 * d:128 * (d + 1)], uk,
                                     ufm_q[:, d, :], "a_uq", d)
                    for hc in range(DC):
                        wb, wbk = wbuf2[hc % 2], f"a_w{hc % 2}"
                        dma("sync", f"wu{hc % 2}", wb,
                            dram["L0_attn_b_Wq"][:, 128 * hc:128 * (hc + 1)]
                            .rearrange("(d p) h -> p d h", p=128), [], [wbk])
                        pb, pk = (p_a, "pb2") if hc % 2 == 0 else (p_b, "pb3")
                        def mm_q(h, wb=wb, pb=pb):
                            for d in range(DC):
                                r = h.matmul(pb[:, 0:128], wb[:, d, :],
                                             ufm_q[:, d, :],
                                             start=(d == 0), stop=(d == DC - 1))
                            return r
                        P.op("tensor", mm_q, [wbk, "a_uq"], [pk])
                        P.op("scalar", lambda h, hc=hc, pb=pb: h.copy(
                             q_fm[:, hc, :], pb[:, 0:128]), [pk], ["a_Q"])

                def qscores(qi):
                    nk = qi // 4 + 1
                    pbs = [p_a, p_b, p_c, p_d]
                    def mm_s(h, nk=nk):
                        for hc in range(DC):
                            for kb in range(nk):
                                r = h.matmul(pbs[kb], q_fm[:, hc, :],
                                             K_fm[:, hc, 512 * kb:512 * (kb + 1)],
                                             start=(hc == 0), stop=(hc == DC - 1))
                        return r
                    P.op("tensor", mm_s, ["a_Q", "a_K"],
                         [f"pb{2 + kb2}" for kb2 in range(nk)])
                    for kb in range(nk):
                        pb, pk = pbs[kb], f"pb{2 + kb}"
                        if kb == nk - 1:
                            moff = 384 - 128 * (qi % 4)
                            P.op("vector", lambda h, kb=kb, pb=pb, moff=moff:
                                 h.tensor_tensor(scr[:, 512 * kb:512 * (kb + 1)],
                                                 pb, maskE[:, moff:moff + 512],
                                                 op=OP.add),
                                 [pk, "maskE"], ["a_scr"])
                        else:
                            P.op("scalar", lambda h, kb=kb, pb=pb: h.copy(
                                 scr[:, 512 * kb:512 * (kb + 1)], pb),
                                 [pk], ["a_scr"])

                def qsm_head(qi):
                    nk = qi // 4 + 1
                    rl = 512 * nk
                    P.op("vector", lambda h, rl=rl: h.reduce_max(
                         mv[:, 0:1], scr[:, 0:rl], axis=AX.X), ["a_scr"], ["mv"])
                    P.op("vector", lambda h: h.tensor_scalar(mv[:, 1:2],
                         mv[:, 0:1], -CSC, None, op0=OP.mult), ["mv"], ["mv"])
                    P.op("scalar", lambda h, rl=rl: h.activation(
                         scr[:, 0:rl], scr[:, 0:rl], AF.Exp, scale=CSC,
                         bias=mv[:, 1:2], accum_out=mv[:, 0:1]),
                         ["a_scr", "mv"], ["a_scr", "mv"])
                    P.op("vector", lambda h: h.reciprocal(mv[:, 1:2],
                         mv[:, 0:1]), ["mv"], ["mv"])
                    P.op("vector", lambda h, rl=rl: h.tensor_scalar(
                         scr[:, 0:rl], scr[:, 0:rl], mv[:, 1:2], 0.0,
                         op0=OP.mult, op1=OP.max), ["a_scr", "mv"], ["a_scr"])

                def q_av(qi):
                    nk = qi // 4 + 1
                    nch = nk * 4
                    for kt in range(nch):
                        wtbuf, wtk = wtb[kt % 2], f"a_wt{kt % 2}"
                        pb, pk = (p_tr, "pb0") if kt % 2 == 0 else (p_tr2, "pb1")
                        P.op("tensor", lambda h, kt=kt, pb=pb: h.transpose(
                             pb, scr[:, 128 * kt:128 * (kt + 1)], eye),
                             ["a_scr", "eye"], [pk])
                        P.op("scalar", lambda h, pb=pb, wtbuf=wtbuf: h.copy(
                             wtbuf, pb), [pk], [wtk])
                        def mm_av(h, kt=kt, wtbuf=wtbuf, nch=nch):
                            for hh in range(2):
                                r = h.matmul(p_av[:, 512 * hh:512 * (hh + 1)],
                                             wtbuf, V_tm[:, kt, 512 * hh:512 * (hh + 1)],
                                             start=(kt == 0), stop=(kt == nch - 1))
                            return r
                        P.op("tensor", mm_av, [wtk, "a_V"], ["pb4", "pb5"])
                    P.op("scalar", lambda h: h.copy(ytm[0], p_av),
                         ["pb4", "pb5"], ["ytm0"])
                    r0 = 1 + base + 128 * qi
                    dma("gpsimd", "xout", X[r0:r0 + 128, :], ytm[0],
                        ["ytm0"], ["X"], accum=True)

                qprep(0)
                for qi in range(16):
                    qscores(qi)
                    qsm_head(qi)
                    if qi + 1 < 16:
                        qprep(qi + 1)
                    q_av(qi)

                P.fence()

        for sl in SUBLAYERS:
            if sl == "mamba":
                emit_mamba()
            elif sl == "ffn":
                emit_ffn(0)
            elif sl == "attn":
                emit_attn()
        dma("sync", "yout", y_out, X[1:NT + 1, :], ["X"], ["yo"])
        P.op("gpsimd", lambda h: h.engine_nop(), ["yo"], [])

        P.materialise(block, sems, dsems)
    return nc


def _host_constants():
    c = {}
    c["c_eye"] = np.eye(128, dtype=np.float32)
    c["c_ltri"] = np.ascontiguousarray(
        np.tril(np.ones((128, 128), np.float32), -1).T)
    c["c_iota640"] = np.ascontiguousarray(
        np.broadcast_to(np.arange(NSLOT, dtype=np.float32), (128, NSLOT)))
    p = np.arange(128, dtype=np.float32)[:, None]
    yv = np.arange(896, dtype=np.float32)[None, :]
    c["c_maskE"] = np.where(yv <= 384 + p, 0.0, -BIG).astype(np.float32)
    c["c_ids"] = (np.arange(32, dtype=np.float32)[None, :] * 128
                  + np.arange(128, dtype=np.float32)[:, None] + 1.0
                  ).astype(np.float32)
    en = np.zeros((16, 16 * 128), np.float32)
    for n in range(16):
        en[n, 128 * n:128 * (n + 1)] = 1.0
    c["c_en"] = en
    ep = (np.arange(128) // 2) % 8
    esel = np.zeros((128, 8), np.float32)
    esel[np.arange(128), ep] = 1.0
    c["c_esel"] = esel
    c["c_eselT"] = np.ascontiguousarray(esel.T)
    e8 = np.zeros((8, 8 * 128), np.float32)
    for e_ in range(8):
        e8[e_, 128 * e_:128 * (e_ + 1)] = 1.0
    c["c_e8"] = e8
    c["c_zrow"] = np.zeros((1, D), np.float32)
    return c


def kernel(**inputs):
    x0 = np.asarray(inputs["L0_attn_a_u"], np.float32)
    shards = [np.ascontiguousarray(x0[2 * c:2 * c + 2].reshape(NT, D))
              for c in range(NCORES)]
    wmap = dict(
        w_xdtT=np.ascontiguousarray(np.asarray(inputs["L0_attn_a_W_xdt"], np.float32).T),
        w_dtT=np.ascontiguousarray(np.asarray(inputs["L0_attn_a_W_dt"], np.float32).T),
        w_bT=np.ascontiguousarray(np.asarray(inputs["L0_attn_a_W_B"], np.float32).T),
        w_cT=np.ascontiguousarray(np.asarray(inputs["L0_attn_a_W_C"], np.float32).T),
    )
    for k in ["L0_attn_a_A_log", "L0_attn_a_b_xdt", "L0_attn_a_b_dt",
              "L0_attn_a_b_B", "L0_attn_a_b_C", "L0_attn_a_D",
              "L0_attn_b_Wk", "L0_attn_b_Wq", "L0_attn_b_Wv",
              "L0_ffn_Wr", "L0_ffn_br", "L0_ffn_W_up", "L0_ffn_b_up",
              "L0_ffn_W_down", "L0_ffn_b_down"]:
        wmap[k] = np.ascontiguousarray(np.asarray(inputs[k], np.float32))
    b_up = np.asarray(inputs["L0_ffn_b_up"], np.float32)
    bup_pad = np.zeros((4, NFC * 128), np.float32)
    bup_pad[:, :FF] = b_up
    wmap["c_bup_fm"] = np.ascontiguousarray(
        bup_pad.reshape(4, NFC, 128).transpose(2, 0, 1))
    wmap["c_bdt_fm"] = np.ascontiguousarray(
        np.asarray(inputs["L0_attn_a_b_dt"], np.float32).reshape(DC, 128).T)
    wmap["c_d_fm"] = np.ascontiguousarray(
        np.asarray(inputs["L0_attn_a_D"], np.float32).reshape(DC, 128).T)
    wmap.update(_host_constants())
    meta = {k: v.shape for k, v in wmap.items()}
    meta["x0"] = (NT, D)
    nc = build_nc(meta)
    in_maps = []
    for c in range(NCORES):
        m = dict(wmap)
        m["x0"] = shards[c]
        in_maps.append(m)
    res = run_bass_kernel_spmd(nc, in_maps, core_ids=list(range(NCORES)))
    return np.concatenate([r["y"].reshape(2, T, D) for r in res.results], axis=0)
